# version 14
# baseline (speedup 1.0000x reference)
import contextlib
import numpy as np
import ml_dtypes
import concourse.bass as bass
import concourse.mybir as mybir
from concourse.bass_utils import run_bass_kernel_spmd

F32 = mybir.dt.float32
BF16 = mybir.dt.bfloat16
AX = mybir.AxisListType.X
OP = mybir.AluOpType
AF = mybir.ActivationFunctionType

N_CORES = 8
T = 2048
D = 1024
NCH = 8
QT = 512
NQT = T // QT
NKC = T // 128
HA = 8
HB = 16
LAMBDA_INIT = 0.5
COS_MAX = 15.0
ALPHA = 0.5
RMS_EPS = 1e-5
LN_EPS = 1e-5
EXPA_SHIFT = -10.0
EXPB_SHIFT = -4.0

ENGINES = ("sync", "vector", "scalar", "tensor", "gpsimd")


class Rec:
    def __init__(self, b, eng):
        self._b = b
        self._eng = eng

    def __getattr__(self, name):
        b, eng = self._b, self._eng

        def call(*args, inc=None, inc_k=1, **kw):
            incs = []
            if inc is not None:
                b.cnt[inc] += inc_k
                incs.append((inc, inc_k))
            b.ops[eng].append((name, args, kw, incs))
        return call


class Builder:
    def __init__(self, nc, stack):
        self.nc = nc
        self.stack = stack
        self.ops = {e: [] for e in ENGINES}
        self.sem_handles = {}
        self.cnt = {}
        self.S = Rec(self, "sync")
        self.V = Rec(self, "vector")
        self.A = Rec(self, "scalar")
        self.P = Rec(self, "tensor")
        self.G = Rec(self, "gpsimd")

    def sem(self, name):
        self.sem_handles[name] = self.stack.enter_context(self.nc.semaphore(name))
        self.cnt[name] = 0

    def wait(self, eng, sem, val=None):
        v = self.cnt[sem] if val is None else val
        self.ops[eng].append(("wait_ge", (self.sem_handles[sem], v), {}, []))

    OWN_SEM = {"sync": "s_dn", "vector": "s_dv", "scalar": "s_ds",
               "tensor": "s_dt", "gpsimd": "s_dg"}

    def drain(self, eng):
        for entry in reversed(self.ops[eng]):
            if entry[0] == "wait_ge":
                return
            if entry[0] in ("drain", "nop"):
                continue
            if entry[3]:
                s0, _ = entry[3][0]
                self.wait(eng, s0)
                return
            own = self.OWN_SEM[eng]
            k = 16 if entry[0] == "dma_start" else 1
            self.cnt[own] += k
            entry[3].append((own, k))
            self.wait(eng, own)
            return

    def inc_last(self, eng, sem, k=1):
        for entry in reversed(self.ops[eng]):
            if entry[0] not in ("wait_ge", "drain", "nop"):
                if entry[3]:
                    s0, _ = entry[3][0]
                    self.wait(eng, s0)
                    self.cnt[sem] += k
                    self.ops[eng].append(("nop", (), {}, [(sem, k)]))
                    return
                k_eff = 16 if entry[0] == "dma_start" else k
                self.cnt[sem] += k_eff
                entry[3].append((sem, k_eff))
                return
        self.cnt[sem] += k
        self.ops[eng].append(("nop", (), {}, [(sem, k)]))

    def emit(self, block):
        sems = self.sem_handles

        def body_for(eng_name):
            oplist = self.ops[eng_name]

            def body(eng):
                for name, args, kw, incs in oplist:
                    if name == "wait_ge":
                        eng.wait_ge(*args)
                        continue
                    r = getattr(eng, name)(*args, **kw)
                    for s_, k_ in incs:
                        r.then_inc(sems[s_], k_)
            return body

        for name in ENGINES:
            getattr(block, name)(body_for(name))


def _rows(t, row0, rstep, nrows, col0, ncols):
    ap = t.ap() if hasattr(t, "ap") and not isinstance(t, bass.AP) else t
    pstride = ap.ap[0][0]
    return bass.AP(ap.tensor, row0 * pstride + col0,
                   [[rstep * pstride, nrows], [1, ncols]])


def _ap(t, offset, dims):
    return bass.AP(t.tensor if isinstance(t, bass.AP) else t, offset, dims)


_UID = [0]


def _u(name):
    _UID[0] += 1
    return f"{name}_{_UID[0]}"


def build_program(n_phases=99, dbg=False):
    _UID[0] = 0
    nc = bass.Bass("TRN2", target_bir_lowering=False, debug=False)
    st = contextlib.ExitStack()
    st.__enter__()
    b = Builder(nc, st)
    S, V, A, P, G = b.S, b.V, b.A, b.P, b.G

    di = {}

    def dram_in(name, shape, dtype=F32):
        di[name] = nc.dram_tensor(name, list(shape), dtype, kind="ExternalInput").ap()

    dram_in("xT", (NCH, 128, T))
    dram_in("WqT", (NCH, D, 128))
    dram_in("WkT", (NCH, D, 128))
    dram_in("WvT", (NCH, 128, D))
    dram_in("WoT", (NCH, D, 128))
    dram_in("bo", (D,))
    dram_in("lamq1", (64,))
    dram_in("lamk1", (64,))
    dram_in("lamq2", (64,))
    dram_in("lamk2", (64,))
    dram_in("dgain", (HA,))
    dram_in("chd", (HA,))
    dram_in("clsr", (1,))
    dram_in("wsub", (128,))
    dram_in("WqkT", (HB, D, 128))
    dram_in("WvB", (NCH, 128, D))
    dram_in("WoB", (NCH, D, 128))
    dram_in("fker", (NCH, 128, D))
    dram_in("fbias", (D,))
    dram_in("fa", (NCH, 128, 16))
    dram_in("fb", (16, D))
    dram_in("cosb", (128, T))
    dram_in("sinbs", (128, T))
    dram_in("tri", (128, 128))
    dram_in("perm", (128, 128))
    dram_in("ident", (128, 128))
    dram_in("identn", (128, 128))

    d_out = nc.dram_tensor("out", [NCH, 128, T], F32, kind="ExternalOutput").ap()
    dbg_outs = {}
    def mk_tap(name, shape):
        dbg_outs[name] = nc.dram_tensor(name, list(shape), F32,
                                        kind="ExternalOutput").ap()
    if dbg:
        for nm in ("dbg_xn", "dbg_q0", "dbg_k0", "dbg_yn0"):
            mk_tap(nm, (128, T))
        mk_tap("dbg_v0", (128, QT))
        mk_tap("dbg_e0", (128, 2, QT))
        mk_tap("dbg_sums", (128, QT))
    d_scr = nc.dram_tensor("scr", [1 << 17], F32, kind="Internal").ap()
    d_scrb = nc.dram_tensor("scrb", [1 << 15], BF16, kind="Internal").ap()
    d_wqb = nc.dram_tensor("wqb", [NCH, D, 128], BF16, kind="Internal").ap()

    for s in ("s_l0", "s_l1", "s_l2", "s_l3",
              "s_ld", "s_pe", "s_ve", "s_se", "s_gd", "s_out", "s_bar",
              "s_lm", "s_dm", "s_g", "s_pa", "s_x", "s_pv", "s_ev", "s_sq",
              "s_cons", "s_wc", "s_sr",
              "s_dn", "s_dv", "s_ds", "s_dt", "s_dg"):
        b.sem(s)

    def sb(name, shape, dtype, stack=None):
        return (stack or st).enter_context(nc.sbuf_tensor(_u(name), list(shape), dtype))

    def ps(name, shape, stack=None):
        return (stack or st).enter_context(nc.psum_tensor(_u(name), list(shape), F32))

    resid = [sb("resid", (128, T), F32) for c in range(NCH)]
    ones_col = sb("ones_col", (128, 1), BF16)
    ones_row = sb("ones_row", (1, 128), BF16)
    lam_row = sb("lam_row", (1, 2), F32)
    cos_row = sb("cos_row", (1, HA), F32)
    gvec = sb("gvec", (128, HA), F32)
    cvec8 = sb("cvec8", (128, HA), F32)
    bo_col = sb("bo_col", (128, NCH), F32)
    fb_col = sb("fb_col", (128, NCH), F32)
    wsc_col = sb("wsc_col", (128, 1), F32)
    lam_col = sb("lam_col", (128, 1), F32)
    tri_bf = sb("tri_bf", (128, 128), BF16)
    perm_bf = sb("perm_bf", (128, 128), BF16)
    id_bf = sb("id_bf", (128, 128), BF16)
    idn_bf = sb("idn_bf", (128, 128), BF16)
    rr = sb("rr", (128, 512), F32)
    rrow = sb("rrow", (1, 1024), F32)
    sgn_col = sb("sgn_col", (128, 1), F32)
    tmpf = sb("tmpf", (128, 512), F32)
    xpf = sb("xpf", (128, 12), F32)
    xpb = sb("xpb", (128, 12), BF16)
    tmpg = sb("tmpg", (128, 512), F32)
    ccol = sb("ccol", (128, 4), F32)

    def ld(rec, dst, src, sem="s_ld", slow=False, fence=True):
        if fence and b.cnt[sem] > 0:
            b.wait(rec._eng, sem)
        if slow:
            rec.dma_start(out=dst, in_=src, inc=sem, inc_k=16,
                          allow_slow_non_contiguous=True)
        else:
            rec.dma_start(out=dst, in_=src, inc=sem, inc_k=16)

    scr_ptr = [0]

    def scr_alloc(n):
        a = scr_ptr[0]
        scr_ptr[0] += n
        assert scr_ptr[0] <= (1 << 16)
        return a

    dbg_stage = sb("dbg_stage", (128, 2 * QT), F32) if dbg else None

    def tap(name, src_ap, dst_ap, ncols, nrows=128):
        if not dbg:
            return
        b.wait("vector", "s_pe")
        b.wait("vector", "s_se")
        b.wait("vector", "s_gd")
        V.tensor_copy(dbg_stage[0:nrows, 0:ncols], src_ap)
        b.drain("vector")
        b.inc_last("vector", "s_ve")
        b.wait("gpsimd", "s_ve")
        ld(G, dst_ap, dbg_stage[0:nrows, 0:ncols], sem="s_gd")
        b.wait("gpsimd", "s_gd")

    def phase_barrier():
        for e in ENGINES:
            b.inc_last(e, "s_bar")
        for e in ENGINES:
            b.wait(e, "s_bar")

    c0_st = contextlib.ExitStack()
    srow = sb("srow", (1, 512), F32, c0_st)
    srow2 = sb("srow2", (1, 512), F32, c0_st)
    tri_f = sb("tri_f", (128, 128), F32, c0_st)
    perm_f = sb("perm_f", (128, 128), F32, c0_st)
    id_f = sb("id_f", (128, 128), F32, c0_st)
    idn_f = sb("idn_f", (128, 128), F32, c0_st)
    for c in range(NCH):
        ld(S if c % 2 == 0 else G, resid[c][:], di["xT"][c],
           sem=f"s_l{c % 4}")
    ld(S, tri_f[:], di["tri"])
    ld(S, perm_f[:], di["perm"])
    ld(S, id_f[:], di["ident"])
    ld(S, idn_f[:], di["identn"])
    ld(S, bo_col[:, :], _ap(di["bo"], 0, [[1, 128], [128, NCH]]), slow=True)
    ld(S, fb_col[:, :], _ap(di["fbias"], 0, [[1, 128], [128, NCH]]), slow=True)
    ld(S, wsc_col[:, 0:1], di["wsub"])
    ld(S, srow[0:1, 0:64], di["lamq1"])
    ld(S, srow[0:1, 64:128], di["lamk1"])
    ld(S, srow[0:1, 128:192], di["lamq2"])
    ld(S, srow[0:1, 192:256], di["lamk2"])
    ld(S, srow[0:1, 256:264], di["chd"])
    ld(S, srow[0:1, 264:265], di["clsr"])
    ld(S, srow[0:1, 272:280], di["dgain"])

    b.wait("vector", "s_ld")
    V.tensor_mul(srow2[0:1, 0:64], srow[0:1, 0:64], srow[0:1, 64:128])
    V.tensor_mul(srow2[0:1, 64:128], srow[0:1, 128:192], srow[0:1, 192:256])
    b.drain("vector")
    V.reduce_sum(srow2[0:1, 128:129], srow2[0:1, 0:64], AX)
    V.reduce_sum(srow2[0:1, 129:130], srow2[0:1, 64:128], AX)
    b.drain("vector")
    b.inc_last("vector", "s_ve")
    b.wait("scalar", "s_ve")
    A.activation(srow2[0:1, 132:134], srow2[0:1, 128:130], AF.Exp, inc="s_se")
    b.wait("vector", "s_se")
    V.tensor_sub(lam_row[0:1, 0:1], srow2[0:1, 132:133], srow2[0:1, 133:134])
    b.drain("vector")
    V.tensor_scalar_add(lam_row[0:1, 0:1], lam_row[0:1, 0:1], LAMBDA_INIT)
    V.reduce_sum(srow2[0:1, 140:141], srow[0:1, 256:264], AX)
    b.drain("vector")
    V.tensor_scalar_mul(srow2[0:1, 140:141], srow2[0:1, 140:141], 1.0 / HA)
    b.drain("vector")
    V.tensor_scalar(srow2[0:1, 144:152], srow[0:1, 256:264], srow2[0:1, 140:141], None,
                    OP.subtract, inc="s_ve")
    b.wait("scalar", "s_ve")
    A.activation(srow2[0:1, 144:152], srow2[0:1, 144:152], AF.Tanh)
    b.drain("scalar")
    A.activation(srow2[0:1, 160:161], srow[0:1, 264:265], AF.Sigmoid, inc="s_se")
    b.wait("vector", "s_se")
    V.tensor_scalar(srow2[0:1, 144:152], srow2[0:1, 144:152], ALPHA, 1.0, OP.mult, OP.add)
    V.tensor_scalar_mul(srow2[0:1, 160:161], srow2[0:1, 160:161], COS_MAX)
    b.drain("vector")
    V.tensor_scalar(cos_row[0:1, :], srow2[0:1, 144:152], srow2[0:1, 160:161], None,
                    OP.mult)
    V.tensor_scalar_mul(wsc_col[:, 0:1], wsc_col[:, 0:1], 1.0 - LAMBDA_INIT)
    V.tensor_copy(tri_bf[:], tri_f[:])
    V.tensor_copy(perm_bf[:], perm_f[:])
    V.tensor_copy(id_bf[:], id_f[:])
    V.tensor_copy(idn_bf[:], idn_f[:])
    V.memset(ones_col[:], 1.0)
    V.memset(ones_row[:], 1.0)
    V.memset(sgn_col[0:64, 0:1], 1.0)
    V.memset(sgn_col[64:128, 0:1], -1.0)
    if dbg:
        V.memset(dbg_stage[:], 0.0)
    V.memset(ccol[:, 0:1], LN_EPS)
    V.memset(ccol[:, 1:2], EXPA_SHIFT)
    V.memset(ccol[:, 2:3], EXPB_SHIFT)
    b.drain("vector")
    b.inc_last("vector", "s_ve")

    a_lam = scr_alloc(32)
    a_tp = scr_alloc(3 * 512)
    b.wait("gpsimd", "s_ve")
    ld(G, d_scr[a_lam:a_lam + 1], lam_row[0:1, 0:1], sem="s_gd")
    ld(G, d_scr[a_lam + 2:a_lam + 10], srow[0:1, 272:280], sem="s_gd")
    ld(G, d_scr[a_lam + 12:a_lam + 20], cos_row[0:1, :], sem="s_gd")
    b.wait("gpsimd", "s_gd")
    ld(G, lam_col[:, 0:1], _ap(d_scr, a_lam, [[0, 128], [1, 1]]), sem="s_gd")
    ld(G, gvec[:, :], _ap(d_scr, a_lam + 2, [[0, 128], [1, HA]]), sem="s_gd")
    ld(G, cvec8[:, :], _ap(d_scr, a_lam + 12, [[0, 128], [1, HA]]), sem="s_gd")

    qst = contextlib.ExitStack()
    qf = [sb("qf", (128, D), F32, qst) for i in range(NCH)]
    qa = sb("qa", (128, D), F32, qst)
    qb_ = sb("qb", (128, D), F32, qst)
    qsg = sb("qsg", (128, D), F32, qst)
    qout = sb("qout", (128, D), BF16, qst)
    am_col = sb("am_col", (128, 4), F32, qst)
    tmpf2 = sb("qtmp2", (128, 512), F32, qst)

    for c in range(NCH):
        ld(S if c % 2 == 0 else G, qf[c][:], di["fker"][c], sem=f"s_l{c % 4}")
    for c in range(NCH):
        b.wait("vector", f"s_l{c % 4}")
        dst = am_col[:, 0:1] if c == 0 else am_col[:, 1:2]
        V.reduce_max(dst, qf[c][:], AX, apply_absolute_value=True)
        b.drain("vector")
        if c > 0:
            V.tensor_max(am_col[:, 0:1], am_col[:, 0:1], am_col[:, 1:2])
            b.drain("vector")
    b.inc_last("vector", "s_ve")
    a_am = scr_alloc(256)
    b.wait("gpsimd", "s_ve")
    ld(G, d_scr[a_am:a_am + 128], am_col[:, 0:1], sem="s_gd")
    b.wait("gpsimd", "s_gd")
    ld(G, srow2[0:1, 300:428], d_scr[a_am:a_am + 128], sem="s_gd")
    b.wait("vector", "s_gd")
    V.reduce_max(srow2[0:1, 430:431], srow2[0:1, 300:428], AX)
    b.drain("vector")
    V.tensor_scalar_mul(srow2[0:1, 431:432], srow2[0:1, 430:431], 1.0 / 7.0)
    b.drain("vector")
    V.reciprocal(srow2[0:1, 432:433], srow2[0:1, 431:432], inc="s_ve")
    b.wait("gpsimd", "s_ve")
    ld(G, d_scr[a_am + 130:a_am + 132], srow2[0:1, 431:433], sem="s_gd")
    b.wait("gpsimd", "s_gd")
    ld(G, am_col[:, 2:3], _ap(d_scr, a_am + 130, [[0, 128], [1, 1]]), sem="s_gd")
    ld(G, am_col[:, 3:4], _ap(d_scr, a_am + 131, [[0, 128], [1, 1]]), sem="s_gd")
    qcols_gd = b.cnt["s_gd"]
    for c in range(NCH):
        buf = qf[c]
        b.wait("vector", "s_gd")
        V.tensor_scalar_mul(qa[:], buf[:], am_col[:, 3:4], inc="s_ve")
        b.wait("scalar", "s_ve")
        A.sign(qsg[:], qa[:])
        b.drain("scalar")
        A.activation(qa[:], qa[:], AF.Abs, inc="s_se")
        b.wait("vector", "s_se")
        V.tensor_scalar(qb_[:], qa[:], 3.5, 4.0, OP.is_ge, OP.mult)
        b.drain("vector")
        V.tensor_sub(qa[:], qa[:], qb_[:])
        b.drain("vector")
        V.tensor_scalar(tmpf[:, 0:512], qa[:, 0:512], 1.5, 2.0, OP.is_ge, OP.mult)
        V.tensor_scalar(tmpf2[:, 0:512], qa[:, 512:1024], 1.5, 2.0, OP.is_ge, OP.mult)
        b.drain("vector")
        V.tensor_add(qb_[:, 0:512], qb_[:, 0:512], tmpf[:, 0:512])
        V.tensor_add(qb_[:, 512:1024], qb_[:, 512:1024], tmpf2[:, 0:512])
        V.tensor_sub(qa[:, 0:512], qa[:, 0:512], tmpf[:, 0:512])
        V.tensor_sub(qa[:, 512:1024], qa[:, 512:1024], tmpf2[:, 0:512])
        b.drain("vector")
        V.tensor_scalar(qa[:], qa[:], 0.5, None, OP.is_ge)
        b.drain("vector")
        V.tensor_add(qb_[:], qb_[:], qa[:])
        b.drain("vector")
        V.tensor_scalar(qb_[:], qb_[:], 7.0, -8.0, OP.min, OP.max)
        b.drain("vector")
        V.tensor_mul(qb_[:], qb_[:], qsg[:])
        b.drain("vector")
        V.tensor_scalar_mul(qout[:], qb_[:], am_col[:, 2:3], inc="s_ve")
        b.wait("gpsimd", "s_ve")
        for m in range(NCH):
            ld(G, d_wqb[m][128 * c:128 * (c + 1), :], qout[:, 128 * m:128 * (m + 1)],
               sem="s_gd")
    quant_gd = b.cnt["s_gd"]
    phase_barrier()
    qst.close()
    c0_st.close()

    def emit_ln(xn, cut=99):
        lst = contextlib.ExitStack()
        stat_st = contextlib.ExitStack()
        px = ps("ln_px", (128, NQT, QT), stat_st)
        pxq = ps("ln_pxq", (128, NQT, QT), stat_st)
        xb = [sb("ln_xb", (128, T), BF16, lst) for _ in range(2)]
        xq = [sb("ln_xq", (128, T), BF16, lst) for _ in range(2)]
        lnrows = sb("ln_rows", (1, 2 * T), BF16, lst)
        v0 = b.cnt["s_ve"]
        for c in range(NCH):
            if c >= 2:
                b.wait("vector", "s_pe")
            V.tensor_copy(xb[c % 2][:], resid[c][:])
            b.drain("vector")
            V.tensor_mul(xq[c % 2][:], xb[c % 2][:], xb[c % 2][:])
            b.drain("vector")
            b.inc_last("vector", "s_ve")
            b.wait("tensor", "s_ve", v0 + c + 1)
            last = (c == NCH - 1)
            for t_ in range(NQT):
                P.matmul(px[0:1, t_, :], ones_col[:, 0:1],
                         xb[c % 2][:, QT * t_:QT * (t_ + 1)], start=(c == 0),
                         stop=last, tile_position=(0, 0))
                P.matmul(pxq[0:1, t_, :], ones_col[:, 0:1],
                         xq[c % 2][:, QT * t_:QT * (t_ + 1)], start=(c == 0),
                         stop=last, tile_position=(0, 0))
            b.inc_last("tensor", "s_pe")
        b.wait("vector", "s_pe")
        if cut < 1:
            stat_st.close()
            lst.close()
            return
        for t_ in range(NQT):
            V.tensor_scalar_mul(rrow[0:1, 0:QT], px[0:1, t_, :], 1.0 / D)
            b.drain("vector")
            V.tensor_mul(rrow[0:1, QT:2 * QT], rrow[0:1, 0:QT], rrow[0:1, 0:QT])
            b.drain("vector")
            V.scalar_tensor_tensor(rrow[0:1, QT:2 * QT], pxq[0:1, t_, :], 1.0 / D,
                                   rrow[0:1, QT:2 * QT], OP.mult, OP.subtract)
            b.drain("vector")
            b.inc_last("vector", "s_ve")
            b.wait("scalar", "s_ve")
            A.activation(rrow[0:1, QT:2 * QT], rrow[0:1, QT:2 * QT], AF.Ln,
                         bias=ccol[0:1, 0:1])
            b.drain("scalar")
            A.activation(rrow[0:1, QT:2 * QT], rrow[0:1, QT:2 * QT], AF.Exp,
                         scale=-0.5, inc="s_se")
            b.wait("vector", "s_se")
            V.tensor_mul(rrow[0:1, 0:QT], rrow[0:1, 0:QT], rrow[0:1, QT:2 * QT])
            b.drain("vector")
            V.tensor_copy(lnrows[0:1, QT * t_:QT * (t_ + 1)], rrow[0:1, QT:2 * QT])
            V.tensor_copy(lnrows[0:1, T + QT * t_:T + QT * (t_ + 1)],
                          rrow[0:1, 0:QT])
            b.drain("vector")
            b.inc_last("vector", "s_ve")
        if cut < 2:
            stat_st.close()
            lst.close()
            return
        stat_st.close()
        pbc = ps("ln_bc", (128, 2, QT), lst)
        for t_ in range(NQT):
            b.wait("tensor", "s_ve")
            b.wait("tensor", "s_dv")
            P.matmul(pbc[:, 0, :], ones_row[0:1, :],
                     lnrows[0:1, QT * t_:QT * (t_ + 1)], start=True, stop=True)
            P.matmul(pbc[:, 1, :], ones_row[0:1, :],
                     lnrows[0:1, T + QT * t_:T + QT * (t_ + 1)], start=True,
                     stop=True, inc="s_pe")
            b.wait("vector", "s_pe")
            for c in range(NCH):
                sl = slice(QT * t_, QT * (t_ + 1))
                V.tensor_mul(tmpf[:, 0:QT], resid[c][:, sl], pbc[:, 0, :])
                b.drain("vector")
                V.tensor_sub(xn[c][:, sl], tmpf[:, 0:QT], pbc[:, 1, :])
                b.drain("vector")
            b.inc_last("vector", "s_dv")
        b.wait("vector", "s_dv")
        b.inc_last("vector", "s_ve")
        lst.close()

    def wdram_chunk(dram, m):
        return _ap(dram, m * D * 128, [[128, 128], [128 * 128, NCH], [1, 128]])

    def emit_proj_fm(wdram, m_list, xin, consumer, cast_scale=None):
        wst = contextlib.ExitStack()
        wf = [sb("pj_wf", (128, QT), F32, wst) for _ in range(2)]
        wb = [sb("pj_wb", (128, D), BF16, wst) for _ in range(2)]
        pbank = [ps("pj_p", (128, QT), wst) for _ in range(2)]
        c0 = b.cnt["s_cons"]
        wc0 = b.cnt["s_wc"]
        g = 0
        for mi, m in enumerate(m_list):
            for hh in range(2):
                hs = 2 * mi + hh
                if hs >= 2:
                    b.wait("sync", "s_wc", wc0 + hs - 1)
                ld(S, _ap(wf[hh].ap(), 0, [[QT, 128], [1, QT]]),
                   _ap(wdram, m * D * 128 + hh * QT * 128,
                       [[128, 128], [128 * 128, NCH // 2], [1, 128]]))
            b.wait("vector", "s_ld")
            b.wait("vector", "s_cons", c0 + 4 * (mi - 1) if mi >= 1 else c0)
            if cast_scale is None:
                V.tensor_copy(wb[mi % 2][:, 0:QT], wf[0][:], inc="s_wc")
                V.tensor_copy(wb[mi % 2][:, QT:D], wf[1][:], inc="s_wc")
            else:
                V.tensor_scalar_mul(wb[mi % 2][:, 0:QT], wf[0][:], cast_scale,
                                    inc="s_wc")
                V.tensor_scalar_mul(wb[mi % 2][:, QT:D], wf[1][:], cast_scale,
                                    inc="s_wc")
            for t_ in range(NQT):
                bank = pbank[g % 2]
                if t_ == 0:
                    b.wait("tensor", "s_wc", wc0 + 2 * (mi + 1))
                b.wait("tensor", "s_cons", c0 + g - 1 if g >= 1 else c0)
                for cc in range(NCH):
                    P.matmul(bank[:], wb[mi % 2][:, 128 * cc:128 * (cc + 1)],
                             xin[cc][:, QT * t_:QT * (t_ + 1)],
                             start=(cc == 0), stop=(cc == NCH - 1))
                b.inc_last("tensor", "s_pe")
                consumer(m, t_, bank)
                g += 1
        b.wait("sync", "s_cons")
        b.wait("tensor", "s_cons")
        b.wait("vector", "s_cons")
        wst.close()

    def consumer_resid_acc(bias_cols):
        def consumer(m, t_, bank):
            b.wait("vector", "s_pe")
            sl = slice(QT * t_, QT * (t_ + 1))
            if bias_cols is None:
                V.tensor_add(resid[m][:, sl], resid[m][:, sl], bank[:], inc="s_cons")
            else:
                V.scalar_tensor_tensor(resid[m][:, sl], bank[:], bias_cols[:, m:m + 1],
                                       resid[m][:, sl], OP.add, OP.add, inc="s_cons")
            b.drain("vector")
            b.inc_last("vector", "s_ve")
        return consumer

    def emit_proj_tm(wdram, xin, evac, halves):
        wst = contextlib.ExitStack()
        wvh = sb("tm_wvh", (128, NCH, QT), BF16, wst)
        wvhf = sb("tm_wvhf", (128, QT), F32, wst)
        pv2 = [ps("tm_pv", (128, QT), wst) for _ in range(2)]
        for half in halves:
            for cc in range(NCH):
                b.wait("sync", "s_wc")
                ld(S, wvhf[:], _ap(wdram, cc * 128 * D + half * QT, [[D, 128], [1, QT]]))
                b.wait("vector", "s_ld")
                V.tensor_copy(wvh[:, cc, :], wvhf[:], inc="s_wc")
            b.wait("tensor", "s_wc")
            c0 = b.cnt["s_cons"]
            for j in range(NKC):
                b.wait("tensor", "s_cons", c0 + j - 1 if j >= 1 else c0)
                for cc in range(NCH):
                    P.matmul(pv2[j % 2][:], xin[cc][:, 128 * j:128 * (j + 1)],
                             wvh[:, cc, :], start=(cc == 0), stop=(cc == NCH - 1))
                b.inc_last("tensor", "s_pe")
                b.wait("vector", "s_pe")
                evac(j, half, pv2[j % 2])
                b.drain("vector")
                b.inc_last("vector", "s_cons")
        b.wait("sync", "s_cons")
        b.wait("tensor", "s_cons")
        wst.close()

    if n_phases >= 0.2:
        ast = contextlib.ExitStack()
        Qh = [sb("aQ", (128, T), BF16, ast) for h in range(HA)]
        xn_st = contextlib.ExitStack()
        xn = [sb("xnA", (128, T), BF16, xn_st) for c in range(NCH)]
        emit_ln(xn, cut=int(round((n_phases - 0.2) * 100)) if n_phases < 0.3 else 99)

        if dbg:
            for tt in range(NQT):
                tap("dbg_xn", xn[0][:, QT * tt:QT * (tt + 1)],
                    dbg_outs["dbg_xn"][:, QT * tt:QT * (tt + 1)], QT)
        for hg in range(2 if n_phases >= 0.4 else 0):
            hst = contextlib.ExitStack()
            Kh = [sb("aK", (128, T), BF16, hst) for i in range(4)]
            Kn2 = [sb("aKn", (128, T), BF16, hst) for i in range(4)]
            Vtm = [sb("aV", (128, QT), BF16, hst) for j in range(NKC)]

            mrange = range(4 * hg, 4 * hg + 4)
            pj_st = contextlib.ExitStack()
            qsq = sb("qsq", (128, QT), BF16, pj_st)
            qrows = sb("qrows", (1, 4 * 2 * QT), BF16, pj_st)
            pssq2 = [ps("pssq", (128, 2, QT), pj_st) for _ in range(2)]
            pbcq = ps("pbcq", (128, QT), pj_st)
            b.wait("vector", "s_ld")

            def run_qk_proj(wdr, dest_of, with_c, neg_of):
                sr0 = b.cnt["s_sr"]
                gq = [0]

                def consumer(m, t_, bank):
                    g = gq[0]
                    sl = slice(QT * t_, QT * (t_ + 1))
                    dest = dest_of(m)
                    b.wait("vector", "s_pe")
                    V.tensor_copy(dest[:, sl], bank[:])
                    b.drain("vector")
                    V.tensor_mul(qsq[:], bank[:], dest[:, sl], inc="s_cons")
                    b.drain("vector")
                    b.inc_last("vector", "s_ve")
                    b.wait("tensor", "s_ve")
                    b.wait("tensor", "s_sr", sr0 + g - 1 if g >= 1 else sr0)
                    P.matmul(pssq2[g % 2][0:1, 0, :], ones_col[0:64, 0:1],
                             qsq[0:64, :], start=True, stop=True,
                             tile_position=(0, 0))
                    P.matmul(pssq2[g % 2][0:1, 1, :], ones_col[64:128, 0:1],
                             qsq[64:128, :], start=True, stop=True,
                             tile_position=(64, 0), inc="s_pe")
                    b.wait("vector", "s_pe")
                    V.tensor_copy(qrows[0:1, (g % 4) * 2 * QT:(g % 4 + 1) * 2 * QT],
                                  pssq2[g % 2][0:1, :, :])
                    b.drain("vector")
                    b.inc_last("vector", "s_sr")
                    gq[0] += 1
                    if t_ == NQT - 1:
                        b.inc_last("vector", "s_ve")
                        b.wait("scalar", "s_ve")
                        A.activation(qrows[0:1, :], qrows[0:1, :], AF.Ln)
                        b.drain("scalar")
                        A.activation(qrows[0:1, :], qrows[0:1, :], AF.Exp,
                                     scale=-0.5, inc="s_se")
                        b.wait("vector", "s_se")
                        V.tensor_scalar_min(qrows[0:1, :], qrows[0:1, :], 1e6)
                        b.drain("vector")
                        if with_c:
                            V.tensor_scalar_mul(qrows[0:1, :], qrows[0:1, :],
                                                cvec8[0:1, m:m + 1])
                            b.drain("vector")
                        b.inc_last("vector", "s_ve")
                        for ta in range(NQT):
                            ga = ta
                            asl = slice(QT * ta, QT * (ta + 1))
                            b.wait("tensor", "s_ve")
                            b.wait("tensor", "s_dv")
                            P.matmul(pbcq[0:64, :], ones_row[0:1, 0:64],
                                     qrows[0:1, ga * 2 * QT:ga * 2 * QT + QT],
                                     start=True, stop=True, tile_position=(0, 0))
                            P.matmul(pbcq[64:128, :], ones_row[0:1, 64:128],
                                     qrows[0:1, ga * 2 * QT + QT:(ga + 1) * 2 * QT],
                                     start=True, stop=True, tile_position=(0, 64),
                                     inc="s_pe")
                            b.wait("vector", "s_pe")
                            V.tensor_mul(dest[:, asl], dest[:, asl], pbcq[:, :])
                            b.drain("vector")
                            if neg_of is not None:
                                V.tensor_scalar_mul(neg_of(m)[:, asl], dest[:, asl],
                                                    sgn_col[:, 0:1])
                                b.drain("vector")
                            b.inc_last("vector", "s_dv")

                emit_proj_fm(wdr, mrange, xn, consumer)
                b.wait("vector", "s_dv")
                b.inc_last("vector", "s_ve")
                b.wait("tensor", "s_ve")

            run_qk_proj(di["WqT"], lambda m: Qh[m], True, None)
            run_qk_proj(di["WkT"], lambda m: Kh[m - 4 * hg], False,
                        lambda m: Kn2[m - 4 * hg])
            b.wait("sync", "s_ve")
            pj_st.close()
            if n_phases < 0.6:
                hst.close()
                continue

            def evac_va(j, half, pv):
                V.tensor_copy(Vtm[j][:, :], pv[:])

            emit_proj_tm(di["WvT"], xn, evac_va, halves=[hg])
            if n_phases < 0.8:
                hst.close()
                continue

            if dbg and hg == 0:
                for tt in range(NQT):
                    tap("dbg_q0", Qh[0][:, QT * tt:QT * (tt + 1)],
                        dbg_outs["dbg_q0"][:, QT * tt:QT * (tt + 1)], QT)
                    tap("dbg_k0", Kh[0][:, QT * tt:QT * (tt + 1)],
                        dbg_outs["dbg_k0"][:, QT * tt:QT * (tt + 1)], QT)
                tap("dbg_v0", Vtm[0][:, :], dbg_outs["dbg_v0"][:, :], QT)
            sc_st = contextlib.ExitStack()
            pL2 = [ps("a_pL", (128, 2, QT), sc_st) for _ in range(2)]
            pD = ps("a_pD", (128, QT), sc_st)
            pY = ps("a_pY", (128, 2, QT), sc_st)
            psums = ps("a_ps", (128, QT), sc_st)
            Gb = [sb("a_G", (128, QT), BF16, sc_st) for _ in range(2)]
            Eb = [sb("a_E", (128, 2, QT), BF16, sc_st) for _ in range(2)]
            Yraw = sb("a_Yr", (128, 2, QT), F32, sc_st)
            y0 = tmpf
            ysq = sb("a_ysq", (128, QT), BF16, sc_st)
            rbA = sb("a_rb", (128, 2, QT), BF16, sc_st)
            rstdb = sb("a_rstdb", (128, QT), BF16, sc_st)
            a_sc = 4 * QT * hg

            gc0 = b.cnt["s_g"]
            lm0 = b.cnt["s_lm"]
            dm0 = b.cnt["s_dm"]
            pa0 = b.cnt["s_pa"]
            x0 = b.cnt["s_x"]
            pv, sq = [], []

            def at(lst, i):
                return lst[i] if 0 <= i < len(lst) else 0

            def aff(base, i):
                return base + i + 1 if i >= 0 else base

            prev = None
            b.wait("tensor", "s_ve")
            b.wait("tensor", "s_gd")
            insts = [(4 * hg + hl, t_) for hl in range(4) for t_ in range(NQT)]
            for xi, (h, t_) in enumerate(insts):
                hl = h - 4 * hg
                tsl = slice(QT * t_, QT * (t_ + 1))
                vs_off = 128 * hl
                for j in range(NKC):
                    jg = xi * NKC + j
                    sB = jg & 1
                    jsl = slice(128 * j, 128 * (j + 1))
                    b.wait("tensor", "s_x", aff(x0, jg - 2))
                    P.matmul(pL2[sB][:, 0, :], Kh[hl][0:64, jsl], Qh[h][0:64, tsl],
                             start=True, stop=False, tile_position=(0, 0))
                    P.matmul(pL2[sB][:, 1, :], Kh[hl][64:128, jsl],
                             Qh[h][64:128, tsl], start=True, stop=False,
                             tile_position=(64, 0), inc="s_lm")
                    b.wait("tensor", "s_g", aff(gc0, jg - 1))
                    if j == 7 and prev is not None:
                        b.wait("tensor", "s_sq", at(sq, xi - 1))
                    P.matmul(pD[:, :], Kn2[hl][:, jsl], Qh[h][:, tsl],
                             start=True, stop=True, tile_position=(0, 0),
                             inc="s_dm")
                    if jg >= 1 and j >= 1:
                        sGp = (jg - 1) & 1
                        P.matmul(pL2[sGp][:, 0, :], id_bf[:, :], Gb[sGp][:],
                                 start=False, stop=True, tile_position=(0, 0))
                        P.matmul(pL2[sGp][:, 1, :], idn_bf[:, :], Gb[sGp][:],
                                 start=False, stop=True, tile_position=(0, 0),
                                 inc="s_pa")
                    if j == 6 and prev is not None:
                        b.wait("tensor", "s_g", gc0 + jg + 1)
                        b.wait("tensor", "s_ve")
                        P.matmul(pD[0:1, :], ones_col[:, 0:1], ysq[:],
                                 start=True, stop=True, tile_position=(0, 0),
                                 inc="s_pv")
                    if j >= 1:
                        b.wait("tensor", "s_x", aff(x0, jg - 1))
                        if j == 1:
                            b.wait("tensor", "s_ev")
                        sBp = (jg - 1) & 1
                        jp = j - 1
                        P.matmul(pY[:, 0, :],
                                 _ap(Vtm[jp].ap(), vs_off, [[QT, 128], [1, 128]]),
                                 Eb[sBp][:, 0, :], start=(jp == 0),
                                 stop=(jp == NKC - 1))
                        P.matmul(pY[:, 1, :],
                                 _ap(Vtm[jp].ap(), vs_off, [[QT, 128], [1, 128]]),
                                 Eb[sBp][:, 1, :], start=(jp == 0),
                                 stop=(jp == NKC - 1))
                        P.matmul(psums[0:1, :], ones_col[:, 0:1], Eb[sBp][:, 0, :],
                                 start=(jp == 0), stop=(jp == NKC - 1),
                                 tile_position=(0, 0))
                        P.matmul(psums[32:33, :], ones_col[:, 0:1], Eb[sBp][:, 1, :],
                                 start=(jp == 0), stop=(jp == NKC - 1),
                                 tile_position=(0, 32), inc="s_pv")
                        pv.append(b.cnt["s_pv"])
                    b.wait("scalar", "s_dm", aff(dm0, jg))
                    b.wait("scalar", "s_pa", aff(pa0, jg - 2))
                    A.activation(Gb[sB][:], pD[:, :], AF.Tanh, scale=gvec[:, h:h + 1],
                                 inc="s_g")

                    if j == 6 and prev is not None:
                        b.wait("vector", "s_pv", b.cnt["s_pv"])
                        b.wait("vector", "s_gd")
                        V.tensor_copy(rr[0:1, 0:QT], pD[0:1, :])
                        b.drain("vector")
                        b.inc_last("vector", "s_sq")
                        while len(sq) < xi:
                            sq.append(b.cnt["s_sq"])
                    if j == 7 and prev is not None:
                        b.wait("gpsimd", "s_sq", at(sq, xi - 1))
                        ld(G, d_scr[a_tp + 2 * QT:a_tp + 3 * QT], rr[0:1, 0:QT],
                           sem="s_gd")
                        b.wait("gpsimd", "s_gd")
                        ld(G, xpf[:, 8:12],
                           _ap(d_scr, a_tp + 2 * QT, [[4, 128], [1, 4]]),
                           sem="s_gd")
                    if j == 8 and prev is not None:
                        b.wait("scalar", "s_gd")
                        A.activation(xpf[:, 8:12], xpf[:, 8:12], AF.Sqrt,
                                     scale=1.0 / 128.0, bias=ccol[:, 0:1],
                                     inc="s_se")
                    if j == 9 and prev is not None:
                        b.wait("vector", "s_se")
                        b.wait("vector", "s_gd")
                        V.reciprocal(xpb[:, 8:12], xpf[:, 8:12])
                        b.drain("vector")
                        b.inc_last("vector", "s_ve")
                    if j == 10 and prev is not None:
                        b.wait("gpsimd", "s_ve")
                        ld(G, _ap(d_scrb, a_sc + 2 * QT, [[4, 128], [1, 4]]),
                           xpb[:, 8:12], sem="s_gd")
                        b.wait("gpsimd", "s_gd")
                        ld(G, rstdb[:, :],
                           _ap(d_scrb, a_sc + 2 * QT, [[1, 1], [0, 128], [1, QT]]),
                           sem="s_gd")
                    if j >= 1:
                        sEp = (jg - 1) & 1
                        b.wait("scalar", "s_pa", aff(pa0, jg - 1))
                        b.wait("scalar", "s_pv", at(pv, jg - 3))
                        A.activation(Eb[sEp][:, :, :], pL2[sEp][:, 0:2, :], AF.Exp,
                                     bias=ccol[:, 1:2], inc="s_x")
                    if prev is not None:
                        ph, pt = prev
                        ptsl = slice(QT * pt, QT * (pt + 1))
                        if j == 4:
                            b.wait("vector", "s_gd")
                            V.tensor_mul(y0[:], Yraw[:, 0, :], rbA[:, 0, :])
                            V.tensor_mul(tmpg[:, 0:QT], Yraw[:, 1, :], rbA[:, 1, :])
                            b.drain("vector")
                            V.tensor_sub(y0[:], y0[:], tmpg[:, 0:QT])
                            b.drain("vector")
                            V.tensor_mul(ysq[:], y0[:], y0[:])
                            b.drain("vector")
                            b.inc_last("vector", "s_ve")
                        if j == 11:
                            b.wait("vector", "s_gd")
                            V.tensor_mul(Qh[ph][:, ptsl], y0[:], rstdb[:, :])
                            b.drain("vector")
                            b.inc_last("vector", "s_ve")
                jg = xi * NKC + NKC - 1
                b.wait("tensor", "s_g", aff(gc0, jg))
                P.matmul(pL2[jg & 1][:, 0, :], id_bf[:, :], Gb[jg & 1][:],
                         start=False, stop=True, tile_position=(0, 0))
                P.matmul(pL2[jg & 1][:, 1, :], idn_bf[:, :], Gb[jg & 1][:],
                         start=False, stop=True, tile_position=(0, 0),
                         inc="s_pa")
                b.wait("scalar", "s_pa", aff(pa0, jg))
                b.wait("scalar", "s_pv", at(pv, jg - 2))
                A.activation(Eb[jg & 1][:, :, :], pL2[jg & 1][:, 0:2, :], AF.Exp,
                             bias=ccol[:, 1:2], inc="s_x")
                b.wait("tensor", "s_x", aff(x0, jg))
                sBp = jg & 1
                jp = NKC - 1
                P.matmul(pY[:, 0, :], _ap(Vtm[jp].ap(), vs_off, [[QT, 128], [1, 128]]),
                         Eb[sBp][:, 0, :], start=False, stop=True)
                P.matmul(pY[:, 1, :], _ap(Vtm[jp].ap(), vs_off, [[QT, 128], [1, 128]]),
                         Eb[sBp][:, 1, :], start=False, stop=True)
                P.matmul(psums[0:1, :], ones_col[:, 0:1], Eb[sBp][:, 0, :],
                         start=False, stop=True, tile_position=(0, 0))
                P.matmul(psums[32:33, :], ones_col[:, 0:1], Eb[sBp][:, 1, :],
                         start=False, stop=True, tile_position=(0, 32), inc="s_pv")
                pv.append(b.cnt["s_pv"])
                b.wait("vector", "s_pv", b.cnt["s_pv"])
                b.wait("vector", "s_gd")
                V.tensor_copy(Yraw[:, 0, :], pY[:, 0, :])
                V.tensor_copy(Yraw[:, 1, :], pY[:, 1, :])
                V.tensor_copy(rr[0:1, 0:QT], psums[0:1, :])
                V.tensor_copy(rr[32:33, 0:QT], psums[32:33, :])
                b.drain("vector")
                b.inc_last("vector", "s_ev")
                b.wait("gpsimd", "s_ev")
                ld(G, d_scr[a_tp:a_tp + QT], rr[0:1, 0:QT], sem="s_gd")
                ld(G, d_scr[a_tp + QT:a_tp + 2 * QT], rr[32:33, 0:QT], sem="s_gd")
                b.wait("gpsimd", "s_gd")
                ld(G, xpf[:, 0:8],
                   _ap(d_scr, a_tp, [[4, 128], [QT, 2], [1, 4]]), sem="s_gd")
                b.wait("vector", "s_gd")
                V.reciprocal(xpb[:, 0:8], xpf[:, 0:8])
                b.drain("vector")
                V.tensor_scalar_mul(xpb[:, 4:8], xpb[:, 4:8], lam_col[:, 0:1])
                b.drain("vector")
                b.inc_last("vector", "s_ve")
                b.wait("gpsimd", "s_ve")
                ld(G, _ap(d_scrb, a_sc, [[4, 128], [QT, 2], [1, 4]]),
                   xpb[:, 0:8], sem="s_gd")
                b.wait("gpsimd", "s_gd")
                ld(G, rbA[:, :, :],
                   _ap(d_scrb, a_sc, [[1, 1], [0, 128], [1, 2 * QT]]), sem="s_gd")
                prev = (h, t_)
            ph, pt = prev
            ptsl = slice(QT * pt, QT * (pt + 1))
            b.wait("vector", "s_gd")
            V.tensor_mul(y0[:], Yraw[:, 0, :], rbA[:, 0, :])
            V.tensor_mul(tmpg[:, 0:QT], Yraw[:, 1, :], rbA[:, 1, :])
            b.drain("vector")
            V.tensor_sub(y0[:], y0[:], tmpg[:, 0:QT])
            b.drain("vector")
            V.tensor_mul(ysq[:], y0[:], y0[:])
            b.drain("vector")
            b.inc_last("vector", "s_ve")
            b.wait("tensor", "s_ve")
            P.matmul(pD[0:1, :], ones_col[:, 0:1], ysq[:], start=True, stop=True,
                     tile_position=(0, 0), inc="s_pv")
            b.wait("vector", "s_pv", b.cnt["s_pv"])
            b.wait("vector", "s_gd")
            V.tensor_copy(rr[0:1, 0:QT], pD[0:1, :])
            b.drain("vector")
            b.inc_last("vector", "s_ve")
            b.wait("gpsimd", "s_ve")
            ld(G, d_scr[a_tp + 2 * QT:a_tp + 3 * QT], rr[0:1, 0:QT], sem="s_gd")
            b.wait("gpsimd", "s_gd")
            ld(G, xpf[:, 8:12], _ap(d_scr, a_tp + 2 * QT, [[4, 128], [1, 4]]),
               sem="s_gd")
            b.wait("scalar", "s_gd")
            A.activation(xpf[:, 8:12], xpf[:, 8:12], AF.Sqrt, scale=1.0 / 128.0,
                         bias=ccol[:, 0:1], inc="s_se")
            b.wait("vector", "s_se")
            V.reciprocal(xpb[:, 8:12], xpf[:, 8:12])
            b.drain("vector")
            b.inc_last("vector", "s_ve")
            b.wait("gpsimd", "s_ve")
            ld(G, _ap(d_scrb, a_sc + 2 * QT, [[4, 128], [1, 4]]), xpb[:, 8:12],
               sem="s_gd")
            b.wait("gpsimd", "s_gd")
            ld(G, rstdb[:, :], _ap(d_scrb, a_sc + 2 * QT, [[1, 1], [0, 128], [1, QT]]),
               sem="s_gd")
            b.wait("vector", "s_gd")
            V.tensor_mul(Qh[ph][:, ptsl], y0[:], rstdb[:, :])
            b.drain("vector")
            b.inc_last("vector", "s_ve")
            phase_barrier()
            sc_st.close()
            hst.close()
        xn_st.close()

        if dbg:
            for tt in range(NQT):
                tap("dbg_yn0", Qh[0][:, QT * tt:QT * (tt + 1)],
                    dbg_outs["dbg_yn0"][:, QT * tt:QT * (tt + 1)], QT)
        if n_phases >= 0.9:
            emit_proj_fm(di["WoT"], range(NCH), Qh, consumer_resid_acc(bo_col),
                         cast_scale=wsc_col[:, 0:1])
        phase_barrier()
        ast.close()

    def emit_ffn():
        fst = contextlib.ExitStack()
        xn = [sb("fxn", (128, T), BF16, fst) for c in range(NCH)]
        emit_ln(xn)
        fab = sb("fab", (128, NCH, 16), BF16, fst)
        fabf = sb("fabf", (128, NCH, 16), F32, fst)
        fbb = sb("fbb", (16, D), BF16, fst)
        fbbf = sb("fbbf", (16, D), F32, fst)
        lora1 = sb("lora1", (16, T), BF16, fst)
        pl = ps("f_pl", (128, QT), fst)
        b.wait("sync", "s_ve")
        for c in range(NCH):
            ld(S, fabf[:, c, :], di["fa"][c])
        ld(S, fbbf[:], di["fb"])
        b.wait("vector", "s_ld")
        V.tensor_copy(fab[:], fabf[:])
        V.tensor_copy(fbb[:], fbbf[:])
        b.drain("vector")
        b.inc_last("vector", "s_ve")
        b.wait("tensor", "s_ve")
        for t_ in range(NQT):
            b.wait("tensor", "s_ve")
            for c in range(NCH):
                P.matmul(pl[0:16, :], fab[:, c, :], xn[c][:, QT * t_:QT * (t_ + 1)],
                         start=(c == 0), stop=(c == NCH - 1))
            b.inc_last("tensor", "s_pe")
            b.wait("vector", "s_pe")
            V.tensor_copy(lora1[:, QT * t_:QT * (t_ + 1)], pl[0:16, :])
            b.drain("vector")
            b.inc_last("vector", "s_ve")
        wqb_s = sb("f_wqb", (128, D), BF16, fst)
        pfd = [ps("f_p", (128, QT), fst) for i in range(2)]
        b.wait("tensor", "s_gd", quant_gd)
        b.wait("tensor", "s_ve")
        b.wait("sync", "s_gd", quant_gd)
        for m in range(NCH):
            b.wait("sync", "s_pe")
            ld(S, wqb_s[:], _ap(d_wqb, m * D * 128,
                                [[128, 128], [128 * 128, NCH], [1, 128]]))
            b.wait("tensor", "s_ld")
            for t_ in range(NQT):
                bank = pfd[t_ % 2]
                tsl = slice(QT * t_, QT * (t_ + 1))
                b.wait("tensor", "s_ve")
                for c in range(NCH):
                    P.matmul(bank[:], wqb_s[:, 128 * c:128 * (c + 1)], xn[c][:, tsl],
                             start=(c == 0), stop=False)
                P.matmul(bank[:], fbb[0:16, 128 * m:128 * (m + 1)], lora1[0:16, tsl],
                         start=False, stop=True, inc="s_pe")
                b.wait("vector", "s_pe")
                V.scalar_tensor_tensor(resid[m][:, tsl], bank[:], fb_col[:, m:m + 1],
                                       resid[m][:, tsl], OP.add, OP.add)
                b.drain("vector")
                b.inc_last("vector", "s_ve")
            b.inc_last("tensor", "s_pe")
        phase_barrier()
        fst.close()

    if n_phases >= 2:
        emit_ffn()

    if n_phases >= 3:
        bst = contextlib.ExitStack()
        qB = [sb("bQ", (128, T), BF16, bst) for c in range(NCH)]
        cosb_b = sb("cosb_b", (128, T), BF16, bst)
        sinb_b = sb("sinb_b", (128, T), BF16, bst)
        xn_st = contextlib.ExitStack()
        xn = [sb("bxn", (128, T), BF16, xn_st) for c in range(NCH)]
        emit_ln(xn)

        rp_st = contextlib.ExitStack()
        cbf = sb("rp_cbf", (128, T), F32, rp_st)
        b.wait("sync", "s_ve")
        ld(S, cbf[:], di["cosb"])
        b.wait("vector", "s_ld")
        V.tensor_copy(cosb_b[:], cbf[:], inc="s_ve")
        b.wait("sync", "s_ve")
        ld(S, cbf[:], di["sinbs"])
        b.wait("vector", "s_ld")
        V.tensor_copy(sinb_b[:], cbf[:], inc="s_ve")
        b.wait("sync", "s_ve")
        rp_st.close()

        for hg in range(2):
            hst = contextlib.ExitStack()
            kB = [sb("bK", (128, T), BF16, hst) for i in range(4)]
            Vaug = [sb("bV", (128, 8, 65), BF16, hst) for j in range(NKC)]
            tb0 = [sb("b_tb0", (128, QT), BF16, hst) for _ in range(2)]
            pswap2 = [ps("b_pswap", (128, QT), hst) for _ in range(2)]

            rope_g = [0]

            def consumer_rope(dest_of):
                def consumer(m, t_, bank):
                    g2 = rope_g[0] % 2
                    rope_g[0] += 1
                    dest = dest_of(m)
                    tsl = slice(QT * t_, QT * (t_ + 1))
                    b.wait("vector", "s_pe")
                    V.tensor_copy(tb0[g2][:], bank[:])
                    b.drain("vector")
                    b.inc_last("vector", "s_ve")
                    b.wait("tensor", "s_ve")
                    P.matmul(pswap2[g2][:], perm_bf[:], tb0[g2][:], start=True,
                             stop=True, inc="s_pe")
                    b.wait("vector", "s_pe")
                    V.tensor_mul(tmpf[:, 0:QT], bank[:], cosb_b[:, tsl], inc="s_cons")
                    V.tensor_mul(tmpg[:, 0:QT], pswap2[g2][:], sinb_b[:, tsl])
                    b.drain("vector")
                    V.tensor_add(dest[:, tsl], tmpf[:, 0:QT], tmpg[:, 0:QT])
                    b.drain("vector")
                    b.inc_last("vector", "s_ve")
                return consumer

            emit_proj_fm(di["WqkT"], range(4 * hg, 4 * hg + 4), xn,
                         consumer_rope(lambda m: qB[m]))
            emit_proj_fm(di["WqkT"], range(8 + 4 * hg, 8 + 4 * hg + 4), xn,
                         consumer_rope(lambda m: kB[m - 8 - 4 * hg]))

            for j in range(NKC):
                V.memset(Vaug[j][:], 0.0)
            b.drain("vector")
            b.inc_last("vector", "s_ve")

            def evac_vb(j, half, pv):
                V.tensor_copy(_ap(Vaug[j].ap(), 0, [[520, 128], [65, 8], [1, 64]]),
                              _ap(pv.ap(), 0, [[QT, 128], [64, 8], [1, 64]]))

            emit_proj_tm(di["WvB"], xn, evac_vb, halves=[hg])
            for j in range(NKC):
                V.memset(_ap(Vaug[j].ap(), 64, [[520, 128], [65, 8], [1, 1]]), 1.0)
            b.drain("vector")
            b.inc_last("vector", "s_ve")

            sb_st = contextlib.ExitStack()
            pS2 = [ps("b_pS", (128, 2, QT), sb_st) for _ in range(2)]
            pYa = ps("b_pYa", (128, QT), sb_st)
            pYb = ps("b_pYb", (128, QT), sb_st)
            EbB = [sb("b_E", (128, 2, QT), BF16, sb_st) for _ in range(2)]
            YrawB = sb("b_Yr", (128, QT), F32, sb_st)
            rbB = sb("b_rb", (128, QT), BF16, sb_st)
            a_sb = 2 * QT * (4 + hg)

            lm0 = b.cnt["s_lm"]
            x0 = b.cnt["s_x"]
            pv, msk = [], {}

            def at(lst, i):
                return lst[i] if 0 <= i < len(lst) else 0

            def aff(base, i):
                return base + i + 1 if i >= 0 else base

            b.wait("tensor", "s_ve")
            instsB = [(4 * hg + ci, t_) for ci in range(4) for t_ in range(NQT)]
            jgc = [0]
            prevB = None
            for xi, (c, t_) in enumerate(instsB):
                ci = c - 4 * hg
                tsl0 = QT * t_
                njc = 4 * t_ + 4
                jlist = list(range(njc))
                for j in jlist:
                    jg = jgc[0]
                    jgc[0] += 1
                    sB = jg & 1
                    q0 = max(0, 128 * j - tsl0)
                    jsl = slice(128 * j, 128 * (j + 1))
                    qsl = slice(tsl0 + q0, tsl0 + QT)
                    diag = (128 * j >= tsl0)
                    b.wait("tensor", "s_x", aff(x0, jg - 2))
                    P.matmul(pS2[sB][:, 0, q0:QT], kB[ci][0:64, jsl], qB[c][0:64, qsl],
                             start=True, stop=True, tile_position=(0, 0))
                    P.matmul(pS2[sB][:, 1, q0:QT], kB[ci][64:128, jsl],
                             qB[c][64:128, qsl], start=True, stop=True,
                             tile_position=(64, 0), inc="s_lm")
                    if j >= 1:
                        pj = j - 1
                        pq0 = max(0, 128 * pj - tsl0)
                        sBp = (jg - 1) & 1
                        wsem = "s_pa" if (jg - 1) in msk else "s_x"
                        wval = msk.get(jg - 1, aff(x0, jg - 1))
                        b.wait("tensor", wsem, wval)
                        if pj == 0:
                            b.wait("tensor", "s_ev")
                        P.matmul(pYa[0:65, pq0:QT], Vaug[pj][:, 2 * ci, :],
                                 EbB[sBp][:, 0, pq0:QT], start=(pj == 0),
                                 stop=(pj == njc - 1))
                        P.matmul(pYb[64:128, pq0:QT], Vaug[pj][:, 2 * ci + 1, 0:64],
                                 EbB[sBp][:, 1, pq0:QT], start=(pj == 0),
                                 stop=(pj == njc - 1), tile_position=(0, 64))
                        P.matmul(pYb[0:1, pq0:QT], ones_col[:, 0:1],
                                 EbB[sBp][:, 1, pq0:QT], start=(pj == 0),
                                 stop=(pj == njc - 1), tile_position=(0, 0),
                                 inc="s_pv")
                        pv.append(b.cnt["s_pv"])
                    b.wait("scalar", "s_lm", aff(lm0, jg))
                    b.wait("scalar", "s_pv", at(pv, jg - 2))
                    A.activation(EbB[sB][:, :, q0:QT], pS2[sB][:, :, q0:QT], AF.Exp,
                                 scale=0.125, bias=ccol[:, 2:3], inc="s_x")
                    if diag:
                        b.wait("vector", "s_x", aff(x0, jg))
                        V.tensor_mul(EbB[sB][:, 0, q0:q0 + 128],
                                     EbB[sB][:, 0, q0:q0 + 128], tri_bf[:, :])
                        V.tensor_mul(EbB[sB][:, 1, q0:q0 + 128],
                                     EbB[sB][:, 1, q0:q0 + 128], tri_bf[:, :],
                                     inc="s_pa")
                        msk[jg] = b.cnt["s_pa"]
                    if prevB is not None:
                        pc, pt = prevB
                        ptsl = slice(QT * pt, QT * (pt + 1))
                        if j == 3:
                            b.wait("vector", "s_gd")
                            V.tensor_mul(qB[pc][0:64, ptsl], YrawB[0:64, :],
                                         rbB[0:64, :])
                            V.tensor_mul(qB[pc][64:128, ptsl], YrawB[64:128, :],
                                         rbB[64:128, :])
                            b.drain("vector")
                            b.inc_last("vector", "s_ve")
                            prevB = None
                jg = jgc[0] - 1
                pj = njc - 1
                pq0 = max(0, 128 * pj - tsl0)
                sBp = jg & 1
                wsem = "s_pa" if jg in msk else "s_x"
                wval = msk.get(jg, aff(x0, jg))
                b.wait("tensor", wsem, wval)
                if pj == 0:
                    b.wait("tensor", "s_ev")
                P.matmul(pYa[0:65, pq0:QT], Vaug[pj][:, 2 * ci, :],
                         EbB[sBp][:, 0, pq0:QT], start=(pj == 0), stop=True)
                P.matmul(pYb[64:128, pq0:QT], Vaug[pj][:, 2 * ci + 1, 0:64],
                         EbB[sBp][:, 1, pq0:QT], start=(pj == 0), stop=True,
                         tile_position=(0, 64))
                P.matmul(pYb[0:1, pq0:QT], ones_col[:, 0:1], EbB[sBp][:, 1, pq0:QT],
                         start=(pj == 0), stop=True, tile_position=(0, 0),
                         inc="s_pv")
                pv.append(b.cnt["s_pv"])
                b.wait("vector", "s_pv", b.cnt["s_pv"])
                b.wait("vector", "s_gd")
                V.tensor_copy(YrawB[0:64, :], pYa[0:64, :])
                V.tensor_copy(YrawB[64:128, :], pYb[64:128, :])
                V.tensor_copy(rr[64:65, 0:QT], pYa[64:65, :])
                V.tensor_copy(rr[0:1, 0:QT], pYb[0:1, :])
                b.drain("vector")
                b.inc_last("vector", "s_ev")
                b.wait("gpsimd", "s_ev")
                ld(G, d_scr[a_tp:a_tp + QT], rr[64:65, 0:QT], sem="s_gd")
                ld(G, d_scr[a_tp + QT:a_tp + 2 * QT], rr[0:1, 0:QT], sem="s_gd")
                b.wait("gpsimd", "s_gd")
                ld(G, xpf[:, 0:8],
                   _ap(d_scr, a_tp, [[4, 128], [QT, 2], [1, 4]]), sem="s_gd")
                b.wait("vector", "s_gd")
                V.reciprocal(xpb[:, 0:8], xpf[:, 0:8])
                b.drain("vector")
                b.inc_last("vector", "s_ve")
                b.wait("gpsimd", "s_ve")
                ld(G, _ap(d_scrb, a_sb, [[4, 128], [QT, 2], [1, 4]]),
                   xpb[:, 0:8], sem="s_gd")
                b.wait("gpsimd", "s_gd")
                ld(G, rbB[:, :], _ap(d_scrb, a_sb, [[QT, 2], [0, 64], [1, QT]]),
                   sem="s_gd")
                prevB = (c, t_)
            pc, pt = prevB
            ptsl = slice(QT * pt, QT * (pt + 1))
            b.wait("vector", "s_gd")
            V.tensor_mul(qB[pc][0:64, ptsl], YrawB[0:64, :], rbB[0:64, :])
            V.tensor_mul(qB[pc][64:128, ptsl], YrawB[64:128, :], rbB[64:128, :])
            b.drain("vector")
            b.inc_last("vector", "s_ve")
            phase_barrier()
            sb_st.close()
            hst.close()
        xn_st.close()

        emit_proj_fm(di["WoB"], range(NCH), qB, consumer_resid_acc(None))
        phase_barrier()
        bst.close()

    if n_phases >= 4:
        emit_ffn()

    b.wait("sync", "s_ve")
    for c in range(NCH):
        S.dma_start(out=d_out[c], in_=resid[c][:], inc="s_out", inc_k=16)
    b.wait("sync", "s_out")

    with nc.allow_low_precision(reason="bf16 softmax-scale broadcasts are within "
                                 "this problem's measured bf16-matmul tolerance"):
        with nc.Block() as block:
            b.emit(block)
    return nc


def _host_prep(inputs):
    f = np.float32

    def blocked_T(w):
        return np.ascontiguousarray(
            w.T.reshape(1024, w.shape[0] // 128, 128).transpose(1, 0, 2)).astype(f)

    def blocked(w):
        n_m = w.shape[1] // 128
        return np.ascontiguousarray(
            w.reshape(w.shape[0], n_m, 128).transpose(1, 0, 2)).astype(f)

    x = np.asarray(inputs["L0_attn_a_x"], f)
    com = {}
    com["WqT"] = blocked_T(np.asarray(inputs["L0_attn_a_Wq"], f))
    com["WkT"] = blocked_T(np.asarray(inputs["L0_attn_a_Wk"], f))
    com["WvT"] = np.ascontiguousarray(
        np.asarray(inputs["L0_attn_a_Wv"], f).T.reshape(8, 128, 1024))
    com["WoT"] = blocked_T(np.asarray(inputs["L0_attn_a_Wo"], f))
    com["bo"] = np.asarray(inputs["L0_attn_a_bo"], f)
    com["lamq1"] = np.asarray(inputs["L0_attn_a_lambda_q1"], f)
    com["lamk1"] = np.asarray(inputs["L0_attn_a_lambda_k1"], f)
    com["lamq2"] = np.asarray(inputs["L0_attn_a_lambda_q2"], f)
    com["lamk2"] = np.asarray(inputs["L0_attn_a_lambda_k2"], f)
    com["dgain"] = np.asarray(inputs["L0_attn_a_delta_gain"], f)
    com["chd"] = np.asarray(inputs["L0_attn_a_cos_head_delta"], f)
    com["clsr"] = np.asarray(inputs["L0_attn_a_cos_logit_scale_raw"], f).reshape(1)
    com["wsub"] = np.asarray(inputs["L0_attn_a_subln_weight"], f)
    wqkv = np.asarray(inputs["L0_attn_b_W_qkv"], f)
    com["WqkT"] = blocked(wqkv[:, 0:2048])
    com["WvB"] = np.ascontiguousarray(wqkv[:, 2048:3072].reshape(8, 128, 1024))
    com["WoB"] = blocked(np.asarray(inputs["L0_attn_b_W_o"], f))
    com["fker"] = np.ascontiguousarray(
        np.asarray(inputs["L0_ffn_kernel"], f).reshape(8, 128, 1024))
    com["fbias"] = np.asarray(inputs["L0_ffn_bias"], f)
    com["fa"] = np.ascontiguousarray(np.asarray(inputs["L0_ffn_a"], f).reshape(8, 128, 16))
    com["fb"] = np.asarray(inputs["L0_ffn_b"], f)

    half = 32
    theta = 1.0 / (10000.0 ** (np.arange(half, dtype=np.float64) / half))
    ang = np.arange(T, dtype=np.float64)[:, None] * theta[None, :]
    cos_ = np.cos(ang).T.astype(f)
    sin_ = np.sin(ang).T.astype(f)
    com["cosb"] = np.ascontiguousarray(np.tile(cos_, (4, 1)))
    sin_block = np.concatenate([-sin_, sin_], axis=0)
    com["sinbs"] = np.ascontiguousarray(np.tile(sin_block, (2, 1)))
    com["tri"] = np.ascontiguousarray(np.triu(np.ones((128, 128), f)))
    pm = np.zeros((128, 128), f)
    for i in range(128):
        pm[i ^ 32, i] = 1.0
    com["perm"] = pm
    com["ident"] = np.ascontiguousarray(np.eye(128, dtype=f))
    com["identn"] = np.ascontiguousarray(-np.eye(128, dtype=f))

    in_maps = []
    for i in range(N_CORES):
        m = dict(com)
        m["xT"] = np.ascontiguousarray(x[i].T).reshape(8, 128, T)
        in_maps.append(m)
    return in_maps


def kernel(**inputs):
    in_maps = _host_prep(inputs)
    nc = build_program()
    res = run_bass_kernel_spmd(nc, in_maps, core_ids=list(range(N_CORES)))
    outs = []
    for i in range(N_CORES):
        o = np.asarray(res.results[i]["out"])
        outs.append(o.reshape(1024, T).T)
    return np.ascontiguousarray(np.stack(outs)).astype(np.float32)


# version 15
# speedup vs baseline: 1.0435x; 1.0435x over previous
import contextlib
import numpy as np
import ml_dtypes
import concourse.bass as bass
import concourse.mybir as mybir
from concourse.bass_utils import run_bass_kernel_spmd

F32 = mybir.dt.float32
BF16 = mybir.dt.bfloat16
AX = mybir.AxisListType.X
OP = mybir.AluOpType
AF = mybir.ActivationFunctionType

N_CORES = 8
T = 2048
D = 1024
NCH = 8
QT = 512
NQT = T // QT
NKC = T // 128
HA = 8
HB = 16
LAMBDA_INIT = 0.5
COS_MAX = 15.0
ALPHA = 0.5
RMS_EPS = 1e-5
LN_EPS = 1e-5
EXPA_SHIFT = -10.0
EXPB_SHIFT = -4.0

ENGINES = ("sync", "vector", "scalar", "tensor", "gpsimd")


class Rec:
    def __init__(self, b, eng):
        self._b = b
        self._eng = eng

    def __getattr__(self, name):
        b, eng = self._b, self._eng

        def call(*args, inc=None, inc_k=1, **kw):
            incs = []
            if inc is not None:
                b.cnt[inc] += inc_k
                incs.append((inc, inc_k))
            b.ops[eng].append((name, args, kw, incs))
        return call


class Builder:
    def __init__(self, nc, stack):
        self.nc = nc
        self.stack = stack
        self.ops = {e: [] for e in ENGINES}
        self.sem_handles = {}
        self.cnt = {}
        self.S = Rec(self, "sync")
        self.V = Rec(self, "vector")
        self.A = Rec(self, "scalar")
        self.P = Rec(self, "tensor")
        self.G = Rec(self, "gpsimd")

    def sem(self, name):
        self.sem_handles[name] = self.stack.enter_context(self.nc.semaphore(name))
        self.cnt[name] = 0

    def wait(self, eng, sem, val=None):
        v = self.cnt[sem] if val is None else val
        self.ops[eng].append(("wait_ge", (self.sem_handles[sem], v), {}, []))

    OWN_SEM = {"sync": "s_dn", "vector": "s_dv", "scalar": "s_ds",
               "tensor": "s_dt", "gpsimd": "s_dg"}

    def drain(self, eng):
        for entry in reversed(self.ops[eng]):
            if entry[0] == "wait_ge":
                return
            if entry[0] in ("drain", "nop"):
                continue
            if entry[3]:
                s0, _ = entry[3][0]
                self.wait(eng, s0)
                return
            own = self.OWN_SEM[eng]
            k = 16 if entry[0] == "dma_start" else 1
            self.cnt[own] += k
            entry[3].append((own, k))
            self.wait(eng, own)
            return

    def inc_last(self, eng, sem, k=1):
        for entry in reversed(self.ops[eng]):
            if entry[0] not in ("wait_ge", "drain", "nop"):
                if entry[3]:
                    s0, _ = entry[3][0]
                    self.wait(eng, s0)
                    self.cnt[sem] += k
                    self.ops[eng].append(("nop", (), {}, [(sem, k)]))
                    return
                k_eff = 16 if entry[0] == "dma_start" else k
                self.cnt[sem] += k_eff
                entry[3].append((sem, k_eff))
                return
        self.cnt[sem] += k
        self.ops[eng].append(("nop", (), {}, [(sem, k)]))

    def emit(self, block):
        sems = self.sem_handles

        def body_for(eng_name):
            oplist = self.ops[eng_name]

            def body(eng):
                for name, args, kw, incs in oplist:
                    if name == "wait_ge":
                        eng.wait_ge(*args)
                        continue
                    r = getattr(eng, name)(*args, **kw)
                    for s_, k_ in incs:
                        r.then_inc(sems[s_], k_)
            return body

        for name in ENGINES:
            getattr(block, name)(body_for(name))


def _rows(t, row0, rstep, nrows, col0, ncols):
    ap = t.ap() if hasattr(t, "ap") and not isinstance(t, bass.AP) else t
    pstride = ap.ap[0][0]
    return bass.AP(ap.tensor, row0 * pstride + col0,
                   [[rstep * pstride, nrows], [1, ncols]])


def _ap(t, offset, dims):
    return bass.AP(t.tensor if isinstance(t, bass.AP) else t, offset, dims)


_UID = [0]


def _u(name):
    _UID[0] += 1
    return f"{name}_{_UID[0]}"


def build_program(n_phases=99, dbg=False):
    _UID[0] = 0
    nc = bass.Bass("TRN2", target_bir_lowering=False, debug=False)
    st = contextlib.ExitStack()
    st.__enter__()
    b = Builder(nc, st)
    S, V, A, P, G = b.S, b.V, b.A, b.P, b.G

    di = {}

    def dram_in(name, shape, dtype=F32):
        di[name] = nc.dram_tensor(name, list(shape), dtype, kind="ExternalInput").ap()

    dram_in("xT", (NCH, 128, T))
    dram_in("WqT", (NCH, D, 128))
    dram_in("WkT", (NCH, D, 128))
    dram_in("WvT", (NCH, 128, D))
    dram_in("WoT", (NCH, D, 128))
    dram_in("bo", (D,))
    dram_in("lamq1", (64,))
    dram_in("lamk1", (64,))
    dram_in("lamq2", (64,))
    dram_in("lamk2", (64,))
    dram_in("dgain", (HA,))
    dram_in("chd", (HA,))
    dram_in("clsr", (1,))
    dram_in("wsub", (128,))
    dram_in("WqkT", (HB, D, 128))
    dram_in("WvB", (NCH, 128, D))
    dram_in("WoB", (NCH, D, 128))
    dram_in("fker", (NCH, 128, D))
    dram_in("fbias", (D,))
    dram_in("fa", (NCH, 128, 16))
    dram_in("fb", (16, D))
    dram_in("cosb", (128, T))
    dram_in("sinbs", (128, T))
    dram_in("tri", (128, 128))
    dram_in("perm", (128, 128))

    d_out = nc.dram_tensor("out", [NCH, 128, T], F32, kind="ExternalOutput").ap()
    dbg_outs = {}
    def mk_tap(name, shape):
        dbg_outs[name] = nc.dram_tensor(name, list(shape), F32,
                                        kind="ExternalOutput").ap()
    if dbg:
        for nm in ("dbg_xn", "dbg_q0", "dbg_k0", "dbg_yn0"):
            mk_tap(nm, (128, T))
        mk_tap("dbg_v0", (128, QT))
        mk_tap("dbg_e0", (128, 2, QT))
        mk_tap("dbg_sums", (128, QT))
    d_scr = nc.dram_tensor("scr", [1 << 17], F32, kind="Internal").ap()
    d_scrb = nc.dram_tensor("scrb", [1 << 15], BF16, kind="Internal").ap()
    d_wqb = nc.dram_tensor("wqb", [NCH, D, 128], BF16, kind="Internal").ap()

    for s in ("s_l0", "s_l1", "s_l2", "s_l3",
              "s_ld", "s_pe", "s_ve", "s_se", "s_gd", "s_out", "s_bar",
              "s_lm", "s_dm", "s_g", "s_pa", "s_x", "s_pv", "s_ev", "s_sq",
              "s_cons", "s_wc", "s_sr",
              "s_dn", "s_dv", "s_ds", "s_dt", "s_dg"):
        b.sem(s)

    def sb(name, shape, dtype, stack=None):
        return (stack or st).enter_context(nc.sbuf_tensor(_u(name), list(shape), dtype))

    def ps(name, shape, stack=None):
        return (stack or st).enter_context(nc.psum_tensor(_u(name), list(shape), F32))

    resid = [sb("resid", (128, T), F32) for c in range(NCH)]
    ones_col = sb("ones_col", (128, 1), BF16)
    ones_row = sb("ones_row", (1, 128), BF16)
    lam_row = sb("lam_row", (1, 2), F32)
    cos_row = sb("cos_row", (1, HA), F32)
    gvec = sb("gvec", (128, HA), F32)
    cvec8 = sb("cvec8", (128, HA), F32)
    bo_col = sb("bo_col", (128, NCH), F32)
    fb_col = sb("fb_col", (128, NCH), F32)
    wsc_col = sb("wsc_col", (128, 1), F32)
    lam_col = sb("lam_col", (128, 1), F32)
    tri_bf = sb("tri_bf", (128, 128), BF16)
    perm_bf = sb("perm_bf", (128, 128), BF16)
    rr = sb("rr", (128, 512), F32)
    rrow = sb("rrow", (1, 1024), F32)
    sgn_col = sb("sgn_col", (128, 1), F32)
    tmpf = sb("tmpf", (128, 512), F32)
    tmpg = sb("tmpg", (128, 512), F32)
    ccol = sb("ccol", (128, 4), F32)

    def ld(rec, dst, src, sem="s_ld", slow=False, fence=True):
        if fence and b.cnt[sem] > 0:
            b.wait(rec._eng, sem)
        if slow:
            rec.dma_start(out=dst, in_=src, inc=sem, inc_k=16,
                          allow_slow_non_contiguous=True)
        else:
            rec.dma_start(out=dst, in_=src, inc=sem, inc_k=16)

    scr_ptr = [0]

    def scr_alloc(n):
        a = scr_ptr[0]
        scr_ptr[0] += n
        assert scr_ptr[0] <= (1 << 16)
        return a

    dbg_stage = sb("dbg_stage", (128, 2 * QT), F32) if dbg else None

    def tap(name, src_ap, dst_ap, ncols, nrows=128):
        if not dbg:
            return
        b.wait("vector", "s_pe")
        b.wait("vector", "s_se")
        b.wait("vector", "s_gd")
        V.tensor_copy(dbg_stage[0:nrows, 0:ncols], src_ap)
        b.drain("vector")
        b.inc_last("vector", "s_ve")
        b.wait("gpsimd", "s_ve")
        ld(G, dst_ap, dbg_stage[0:nrows, 0:ncols], sem="s_gd")
        b.wait("gpsimd", "s_gd")

    def phase_barrier():
        for e in ENGINES:
            b.inc_last(e, "s_bar")
        for e in ENGINES:
            b.wait(e, "s_bar")

    c0_st = contextlib.ExitStack()
    srow = sb("srow", (1, 512), F32, c0_st)
    srow2 = sb("srow2", (1, 512), F32, c0_st)
    tri_f = sb("tri_f", (128, 128), F32, c0_st)
    perm_f = sb("perm_f", (128, 128), F32, c0_st)
    for c in range(NCH):
        ld(S if c % 2 == 0 else G, resid[c][:], di["xT"][c],
           sem=f"s_l{c % 4}")
    ld(S, tri_f[:], di["tri"])
    ld(S, perm_f[:], di["perm"])
    ld(S, bo_col[:, :], _ap(di["bo"], 0, [[1, 128], [128, NCH]]), slow=True)
    ld(S, fb_col[:, :], _ap(di["fbias"], 0, [[1, 128], [128, NCH]]), slow=True)
    ld(S, wsc_col[:, 0:1], di["wsub"])
    ld(S, srow[0:1, 0:64], di["lamq1"])
    ld(S, srow[0:1, 64:128], di["lamk1"])
    ld(S, srow[0:1, 128:192], di["lamq2"])
    ld(S, srow[0:1, 192:256], di["lamk2"])
    ld(S, srow[0:1, 256:264], di["chd"])
    ld(S, srow[0:1, 264:265], di["clsr"])
    ld(S, srow[0:1, 272:280], di["dgain"])

    b.wait("vector", "s_ld")
    V.tensor_mul(srow2[0:1, 0:64], srow[0:1, 0:64], srow[0:1, 64:128])
    V.tensor_mul(srow2[0:1, 64:128], srow[0:1, 128:192], srow[0:1, 192:256])
    b.drain("vector")
    V.reduce_sum(srow2[0:1, 128:129], srow2[0:1, 0:64], AX)
    V.reduce_sum(srow2[0:1, 129:130], srow2[0:1, 64:128], AX)
    b.drain("vector")
    b.inc_last("vector", "s_ve")
    b.wait("scalar", "s_ve")
    A.activation(srow2[0:1, 132:134], srow2[0:1, 128:130], AF.Exp, inc="s_se")
    b.wait("vector", "s_se")
    V.tensor_sub(lam_row[0:1, 0:1], srow2[0:1, 132:133], srow2[0:1, 133:134])
    b.drain("vector")
    V.tensor_scalar_add(lam_row[0:1, 0:1], lam_row[0:1, 0:1], LAMBDA_INIT)
    V.reduce_sum(srow2[0:1, 140:141], srow[0:1, 256:264], AX)
    b.drain("vector")
    V.tensor_scalar_mul(srow2[0:1, 140:141], srow2[0:1, 140:141], 1.0 / HA)
    b.drain("vector")
    V.tensor_scalar(srow2[0:1, 144:152], srow[0:1, 256:264], srow2[0:1, 140:141], None,
                    OP.subtract, inc="s_ve")
    b.wait("scalar", "s_ve")
    A.activation(srow2[0:1, 144:152], srow2[0:1, 144:152], AF.Tanh)
    b.drain("scalar")
    A.activation(srow2[0:1, 160:161], srow[0:1, 264:265], AF.Sigmoid, inc="s_se")
    b.wait("vector", "s_se")
    V.tensor_scalar(srow2[0:1, 144:152], srow2[0:1, 144:152], ALPHA, 1.0, OP.mult, OP.add)
    V.tensor_scalar_mul(srow2[0:1, 160:161], srow2[0:1, 160:161], COS_MAX)
    b.drain("vector")
    V.tensor_scalar(cos_row[0:1, :], srow2[0:1, 144:152], srow2[0:1, 160:161], None,
                    OP.mult)
    V.tensor_scalar_mul(wsc_col[:, 0:1], wsc_col[:, 0:1], 1.0 - LAMBDA_INIT)
    V.tensor_copy(tri_bf[:], tri_f[:])
    V.tensor_copy(perm_bf[:], perm_f[:])
    V.memset(ones_col[:], 1.0)
    V.memset(ones_row[:], 1.0)
    V.memset(sgn_col[0:64, 0:1], 1.0)
    V.memset(sgn_col[64:128, 0:1], -1.0)
    if dbg:
        V.memset(dbg_stage[:], 0.0)
    V.memset(ccol[:, 0:1], LN_EPS)
    V.memset(ccol[:, 1:2], EXPA_SHIFT)
    V.memset(ccol[:, 2:3], EXPB_SHIFT)
    b.drain("vector")
    b.inc_last("vector", "s_ve")

    a_lam = scr_alloc(32)
    b.wait("gpsimd", "s_ve")
    ld(G, d_scr[a_lam:a_lam + 1], lam_row[0:1, 0:1], sem="s_gd")
    ld(G, d_scr[a_lam + 2:a_lam + 10], srow[0:1, 272:280], sem="s_gd")
    ld(G, d_scr[a_lam + 12:a_lam + 20], cos_row[0:1, :], sem="s_gd")
    b.wait("gpsimd", "s_gd")
    ld(G, lam_col[:, 0:1], _ap(d_scr, a_lam, [[0, 128], [1, 1]]), sem="s_gd")
    ld(G, gvec[:, :], _ap(d_scr, a_lam + 2, [[0, 128], [1, HA]]), sem="s_gd")
    ld(G, cvec8[:, :], _ap(d_scr, a_lam + 12, [[0, 128], [1, HA]]), sem="s_gd")

    qst = contextlib.ExitStack()
    qf = [sb("qf", (128, D), F32, qst) for i in range(NCH)]
    qa = sb("qa", (128, D), F32, qst)
    qb_ = sb("qb", (128, D), F32, qst)
    qsg = sb("qsg", (128, D), F32, qst)
    qout = sb("qout", (128, D), BF16, qst)
    am_col = sb("am_col", (128, 4), F32, qst)
    tmpf2 = sb("qtmp2", (128, 512), F32, qst)

    for c in range(NCH):
        ld(S if c % 2 == 0 else G, qf[c][:], di["fker"][c], sem=f"s_l{c % 4}")
    for c in range(NCH):
        b.wait("vector", f"s_l{c % 4}")
        dst = am_col[:, 0:1] if c == 0 else am_col[:, 1:2]
        V.reduce_max(dst, qf[c][:], AX, apply_absolute_value=True)
        b.drain("vector")
        if c > 0:
            V.tensor_max(am_col[:, 0:1], am_col[:, 0:1], am_col[:, 1:2])
            b.drain("vector")
    b.inc_last("vector", "s_ve")
    a_am = scr_alloc(256)
    b.wait("gpsimd", "s_ve")
    ld(G, d_scr[a_am:a_am + 128], am_col[:, 0:1], sem="s_gd")
    b.wait("gpsimd", "s_gd")
    ld(G, srow2[0:1, 300:428], d_scr[a_am:a_am + 128], sem="s_gd")
    b.wait("vector", "s_gd")
    V.reduce_max(srow2[0:1, 430:431], srow2[0:1, 300:428], AX)
    b.drain("vector")
    V.tensor_scalar_mul(srow2[0:1, 431:432], srow2[0:1, 430:431], 1.0 / 7.0)
    b.drain("vector")
    V.reciprocal(srow2[0:1, 432:433], srow2[0:1, 431:432], inc="s_ve")
    b.wait("gpsimd", "s_ve")
    ld(G, d_scr[a_am + 130:a_am + 132], srow2[0:1, 431:433], sem="s_gd")
    b.wait("gpsimd", "s_gd")
    ld(G, am_col[:, 2:3], _ap(d_scr, a_am + 130, [[0, 128], [1, 1]]), sem="s_gd")
    ld(G, am_col[:, 3:4], _ap(d_scr, a_am + 131, [[0, 128], [1, 1]]), sem="s_gd")
    qcols_gd = b.cnt["s_gd"]
    for c in range(NCH):
        buf = qf[c]
        b.wait("vector", "s_gd")
        V.tensor_scalar_mul(qa[:], buf[:], am_col[:, 3:4], inc="s_ve")
        b.wait("scalar", "s_ve")
        A.sign(qsg[:], qa[:])
        b.drain("scalar")
        A.activation(qa[:], qa[:], AF.Abs, inc="s_se")
        b.wait("vector", "s_se")
        V.tensor_scalar(qb_[:], qa[:], 3.5, 4.0, OP.is_ge, OP.mult)
        b.drain("vector")
        V.tensor_sub(qa[:], qa[:], qb_[:])
        b.drain("vector")
        V.tensor_scalar(tmpf[:, 0:512], qa[:, 0:512], 1.5, 2.0, OP.is_ge, OP.mult)
        V.tensor_scalar(tmpf2[:, 0:512], qa[:, 512:1024], 1.5, 2.0, OP.is_ge, OP.mult)
        b.drain("vector")
        V.tensor_add(qb_[:, 0:512], qb_[:, 0:512], tmpf[:, 0:512])
        V.tensor_add(qb_[:, 512:1024], qb_[:, 512:1024], tmpf2[:, 0:512])
        V.tensor_sub(qa[:, 0:512], qa[:, 0:512], tmpf[:, 0:512])
        V.tensor_sub(qa[:, 512:1024], qa[:, 512:1024], tmpf2[:, 0:512])
        b.drain("vector")
        V.tensor_scalar(qa[:], qa[:], 0.5, None, OP.is_ge)
        b.drain("vector")
        V.tensor_add(qb_[:], qb_[:], qa[:])
        b.drain("vector")
        V.tensor_scalar(qb_[:], qb_[:], 7.0, -8.0, OP.min, OP.max)
        b.drain("vector")
        V.tensor_mul(qb_[:], qb_[:], qsg[:])
        b.drain("vector")
        V.tensor_scalar_mul(qout[:], qb_[:], am_col[:, 2:3], inc="s_ve")
        b.wait("gpsimd", "s_ve")
        for m in range(NCH):
            ld(G, d_wqb[m][128 * c:128 * (c + 1), :], qout[:, 128 * m:128 * (m + 1)],
               sem="s_gd")
    quant_gd = b.cnt["s_gd"]
    phase_barrier()
    qst.close()
    c0_st.close()

    def emit_ln(xn, cut=99):
        lst = contextlib.ExitStack()
        stat_st = contextlib.ExitStack()
        px = ps("ln_px", (128, NQT, QT), stat_st)
        pxq = ps("ln_pxq", (128, NQT, QT), stat_st)
        xb = [sb("ln_xb", (128, T), BF16, lst) for _ in range(2)]
        xq = [sb("ln_xq", (128, T), BF16, lst) for _ in range(2)]
        lnrows = sb("ln_rows", (1, 2 * T), BF16, lst)
        v0 = b.cnt["s_ve"]
        for c in range(NCH):
            if c >= 2:
                b.wait("vector", "s_pe")
            V.tensor_copy(xb[c % 2][:], resid[c][:])
            b.drain("vector")
            V.tensor_mul(xq[c % 2][:], xb[c % 2][:], xb[c % 2][:])
            b.drain("vector")
            b.inc_last("vector", "s_ve")
            b.wait("tensor", "s_ve", v0 + c + 1)
            last = (c == NCH - 1)
            for t_ in range(NQT):
                P.matmul(px[0:1, t_, :], ones_col[:, 0:1],
                         xb[c % 2][:, QT * t_:QT * (t_ + 1)], start=(c == 0),
                         stop=last, tile_position=(0, 0))
                P.matmul(pxq[0:1, t_, :], ones_col[:, 0:1],
                         xq[c % 2][:, QT * t_:QT * (t_ + 1)], start=(c == 0),
                         stop=last, tile_position=(0, 0))
            b.inc_last("tensor", "s_pe")
        b.wait("vector", "s_pe")
        if cut < 1:
            stat_st.close()
            lst.close()
            return
        for t_ in range(NQT):
            V.tensor_scalar_mul(rrow[0:1, 0:QT], px[0:1, t_, :], 1.0 / D)
            b.drain("vector")
            V.tensor_mul(rrow[0:1, QT:2 * QT], rrow[0:1, 0:QT], rrow[0:1, 0:QT])
            b.drain("vector")
            V.scalar_tensor_tensor(rrow[0:1, QT:2 * QT], pxq[0:1, t_, :], 1.0 / D,
                                   rrow[0:1, QT:2 * QT], OP.mult, OP.subtract)
            b.drain("vector")
            b.inc_last("vector", "s_ve")
            b.wait("scalar", "s_ve")
            A.activation(rrow[0:1, QT:2 * QT], rrow[0:1, QT:2 * QT], AF.Ln,
                         bias=ccol[0:1, 0:1])
            b.drain("scalar")
            A.activation(rrow[0:1, QT:2 * QT], rrow[0:1, QT:2 * QT], AF.Exp,
                         scale=-0.5, inc="s_se")
            b.wait("vector", "s_se")
            V.tensor_mul(rrow[0:1, 0:QT], rrow[0:1, 0:QT], rrow[0:1, QT:2 * QT])
            b.drain("vector")
            V.tensor_copy(lnrows[0:1, QT * t_:QT * (t_ + 1)], rrow[0:1, QT:2 * QT])
            V.tensor_copy(lnrows[0:1, T + QT * t_:T + QT * (t_ + 1)],
                          rrow[0:1, 0:QT])
            b.drain("vector")
            b.inc_last("vector", "s_ve")
        if cut < 2:
            stat_st.close()
            lst.close()
            return
        stat_st.close()
        pbc = ps("ln_bc", (128, 2, QT), lst)
        for t_ in range(NQT):
            b.wait("tensor", "s_ve")
            b.wait("tensor", "s_dv")
            P.matmul(pbc[:, 0, :], ones_row[0:1, :],
                     lnrows[0:1, QT * t_:QT * (t_ + 1)], start=True, stop=True)
            P.matmul(pbc[:, 1, :], ones_row[0:1, :],
                     lnrows[0:1, T + QT * t_:T + QT * (t_ + 1)], start=True,
                     stop=True, inc="s_pe")
            b.wait("vector", "s_pe")
            for c in range(NCH):
                sl = slice(QT * t_, QT * (t_ + 1))
                V.tensor_mul(tmpf[:, 0:QT], resid[c][:, sl], pbc[:, 0, :])
                b.drain("vector")
                V.tensor_sub(xn[c][:, sl], tmpf[:, 0:QT], pbc[:, 1, :])
                b.drain("vector")
            b.inc_last("vector", "s_dv")
        b.wait("vector", "s_dv")
        b.inc_last("vector", "s_ve")
        lst.close()

    def wdram_chunk(dram, m):
        return _ap(dram, m * D * 128, [[128, 128], [128 * 128, NCH], [1, 128]])

    def emit_proj_fm(wdram, m_list, xin, consumer, cast_scale=None):
        wst = contextlib.ExitStack()
        wf = [sb("pj_wf", (128, QT), F32, wst) for _ in range(2)]
        wb = [sb("pj_wb", (128, D), BF16, wst) for _ in range(2)]
        pbank = [ps("pj_p", (128, QT), wst) for _ in range(2)]
        c0 = b.cnt["s_cons"]
        wc0 = b.cnt["s_wc"]
        g = 0
        for mi, m in enumerate(m_list):
            for hh in range(2):
                hs = 2 * mi + hh
                if hs >= 2:
                    b.wait("sync", "s_wc", wc0 + hs - 1)
                ld(S, _ap(wf[hh].ap(), 0, [[QT, 128], [1, QT]]),
                   _ap(wdram, m * D * 128 + hh * QT * 128,
                       [[128, 128], [128 * 128, NCH // 2], [1, 128]]))
            b.wait("vector", "s_ld")
            b.wait("vector", "s_cons", c0 + 4 * (mi - 1) if mi >= 1 else c0)
            if cast_scale is None:
                V.tensor_copy(wb[mi % 2][:, 0:QT], wf[0][:], inc="s_wc")
                V.tensor_copy(wb[mi % 2][:, QT:D], wf[1][:], inc="s_wc")
            else:
                V.tensor_scalar_mul(wb[mi % 2][:, 0:QT], wf[0][:], cast_scale,
                                    inc="s_wc")
                V.tensor_scalar_mul(wb[mi % 2][:, QT:D], wf[1][:], cast_scale,
                                    inc="s_wc")
            for t_ in range(NQT):
                bank = pbank[g % 2]
                if t_ == 0:
                    b.wait("tensor", "s_wc", wc0 + 2 * (mi + 1))
                b.wait("tensor", "s_cons", c0 + g - 1 if g >= 1 else c0)
                for cc in range(NCH):
                    P.matmul(bank[:], wb[mi % 2][:, 128 * cc:128 * (cc + 1)],
                             xin[cc][:, QT * t_:QT * (t_ + 1)],
                             start=(cc == 0), stop=(cc == NCH - 1))
                b.inc_last("tensor", "s_pe")
                consumer(m, t_, bank)
                g += 1
        b.wait("sync", "s_cons")
        b.wait("tensor", "s_cons")
        b.wait("vector", "s_cons")
        wst.close()

    def consumer_resid_acc(bias_cols):
        def consumer(m, t_, bank):
            b.wait("vector", "s_pe")
            sl = slice(QT * t_, QT * (t_ + 1))
            if bias_cols is None:
                V.tensor_add(resid[m][:, sl], resid[m][:, sl], bank[:], inc="s_cons")
            else:
                V.scalar_tensor_tensor(resid[m][:, sl], bank[:], bias_cols[:, m:m + 1],
                                       resid[m][:, sl], OP.add, OP.add, inc="s_cons")
            b.drain("vector")
            b.inc_last("vector", "s_ve")
        return consumer

    def emit_proj_tm(wdram, xin, evac, halves):
        wst = contextlib.ExitStack()
        wvh = sb("tm_wvh", (128, NCH, QT), BF16, wst)
        wvhf = sb("tm_wvhf", (128, QT), F32, wst)
        pv2 = [ps("tm_pv", (128, QT), wst) for _ in range(2)]
        for half in halves:
            for cc in range(NCH):
                b.wait("sync", "s_wc")
                ld(S, wvhf[:], _ap(wdram, cc * 128 * D + half * QT, [[D, 128], [1, QT]]))
                b.wait("vector", "s_ld")
                V.tensor_copy(wvh[:, cc, :], wvhf[:], inc="s_wc")
            b.wait("tensor", "s_wc")
            c0 = b.cnt["s_cons"]
            for j in range(NKC):
                b.wait("tensor", "s_cons", c0 + j - 1 if j >= 1 else c0)
                for cc in range(NCH):
                    P.matmul(pv2[j % 2][:], xin[cc][:, 128 * j:128 * (j + 1)],
                             wvh[:, cc, :], start=(cc == 0), stop=(cc == NCH - 1))
                b.inc_last("tensor", "s_pe")
                b.wait("vector", "s_pe")
                evac(j, half, pv2[j % 2])
                b.drain("vector")
                b.inc_last("vector", "s_cons")
        b.wait("sync", "s_cons")
        b.wait("tensor", "s_cons")
        wst.close()

    if n_phases >= 0.2:
        ast = contextlib.ExitStack()
        Qh = [sb("aQ", (128, T), BF16, ast) for h in range(HA)]
        xn_st = contextlib.ExitStack()
        xn = [sb("xnA", (128, T), BF16, xn_st) for c in range(NCH)]
        emit_ln(xn, cut=int(round((n_phases - 0.2) * 100)) if n_phases < 0.3 else 99)

        if dbg:
            for tt in range(NQT):
                tap("dbg_xn", xn[0][:, QT * tt:QT * (tt + 1)],
                    dbg_outs["dbg_xn"][:, QT * tt:QT * (tt + 1)], QT)
        for hg in range(2 if n_phases >= 0.4 else 0):
            hst = contextlib.ExitStack()
            Kh = [sb("aK", (128, T), BF16, hst) for i in range(4)]
            Kn2 = [sb("aKn", (128, T), BF16, hst) for i in range(4)]
            Vtm = [sb("aV", (128, QT), BF16, hst) for j in range(NKC)]

            mrange = range(4 * hg, 4 * hg + 4)
            pj_st = contextlib.ExitStack()
            qsq = sb("qsq", (128, QT), BF16, pj_st)
            qrows = sb("qrows", (1, 4 * 2 * QT), BF16, pj_st)
            pssq2 = [ps("pssq", (128, 2, QT), pj_st) for _ in range(2)]
            pbcq = ps("pbcq", (128, QT), pj_st)
            b.wait("vector", "s_ld")

            def run_qk_proj(wdr, dest_of, with_c, neg_of):
                sr0 = b.cnt["s_sr"]
                gq = [0]

                def consumer(m, t_, bank):
                    g = gq[0]
                    sl = slice(QT * t_, QT * (t_ + 1))
                    dest = dest_of(m)
                    b.wait("vector", "s_pe")
                    V.tensor_copy(dest[:, sl], bank[:])
                    b.drain("vector")
                    V.tensor_mul(qsq[:], bank[:], dest[:, sl], inc="s_cons")
                    b.drain("vector")
                    b.inc_last("vector", "s_ve")
                    b.wait("tensor", "s_ve")
                    b.wait("tensor", "s_sr", sr0 + g - 1 if g >= 1 else sr0)
                    P.matmul(pssq2[g % 2][0:1, 0, :], ones_col[0:64, 0:1],
                             qsq[0:64, :], start=True, stop=True,
                             tile_position=(0, 0))
                    P.matmul(pssq2[g % 2][0:1, 1, :], ones_col[64:128, 0:1],
                             qsq[64:128, :], start=True, stop=True,
                             tile_position=(64, 0), inc="s_pe")
                    b.wait("vector", "s_pe")
                    V.tensor_copy(qrows[0:1, (g % 4) * 2 * QT:(g % 4 + 1) * 2 * QT],
                                  pssq2[g % 2][0:1, :, :])
                    b.drain("vector")
                    b.inc_last("vector", "s_sr")
                    gq[0] += 1
                    if t_ == NQT - 1:
                        b.inc_last("vector", "s_ve")
                        b.wait("scalar", "s_ve")
                        A.activation(qrows[0:1, :], qrows[0:1, :], AF.Ln)
                        b.drain("scalar")
                        A.activation(qrows[0:1, :], qrows[0:1, :], AF.Exp,
                                     scale=-0.5, inc="s_se")
                        b.wait("vector", "s_se")
                        V.tensor_scalar_min(qrows[0:1, :], qrows[0:1, :], 1e6)
                        b.drain("vector")
                        if with_c:
                            V.tensor_scalar_mul(qrows[0:1, :], qrows[0:1, :],
                                                cvec8[0:1, m:m + 1])
                            b.drain("vector")
                        b.inc_last("vector", "s_ve")
                        for ta in range(NQT):
                            ga = ta
                            asl = slice(QT * ta, QT * (ta + 1))
                            b.wait("tensor", "s_ve")
                            b.wait("tensor", "s_dv")
                            P.matmul(pbcq[0:64, :], ones_row[0:1, 0:64],
                                     qrows[0:1, ga * 2 * QT:ga * 2 * QT + QT],
                                     start=True, stop=True, tile_position=(0, 0))
                            P.matmul(pbcq[64:128, :], ones_row[0:1, 64:128],
                                     qrows[0:1, ga * 2 * QT + QT:(ga + 1) * 2 * QT],
                                     start=True, stop=True, tile_position=(0, 64),
                                     inc="s_pe")
                            b.wait("vector", "s_pe")
                            V.tensor_mul(dest[:, asl], dest[:, asl], pbcq[:, :])
                            b.drain("vector")
                            if neg_of is not None:
                                V.tensor_scalar_mul(neg_of(m)[:, asl], dest[:, asl],
                                                    sgn_col[:, 0:1])
                                b.drain("vector")
                            b.inc_last("vector", "s_dv")

                emit_proj_fm(wdr, mrange, xn, consumer)
                b.wait("vector", "s_dv")
                b.inc_last("vector", "s_ve")
                b.wait("tensor", "s_ve")

            run_qk_proj(di["WqT"], lambda m: Qh[m], True, None)
            run_qk_proj(di["WkT"], lambda m: Kh[m - 4 * hg], False,
                        lambda m: Kn2[m - 4 * hg])
            b.wait("sync", "s_ve")
            pj_st.close()
            if n_phases < 0.6:
                hst.close()
                continue

            def evac_va(j, half, pv):
                V.tensor_copy(Vtm[j][:, :], pv[:])

            emit_proj_tm(di["WvT"], xn, evac_va, halves=[hg])
            if n_phases < 0.8:
                hst.close()
                continue

            if dbg and hg == 0:
                for tt in range(NQT):
                    tap("dbg_q0", Qh[0][:, QT * tt:QT * (tt + 1)],
                        dbg_outs["dbg_q0"][:, QT * tt:QT * (tt + 1)], QT)
                    tap("dbg_k0", Kh[0][:, QT * tt:QT * (tt + 1)],
                        dbg_outs["dbg_k0"][:, QT * tt:QT * (tt + 1)], QT)
                tap("dbg_v0", Vtm[0][:, :], dbg_outs["dbg_v0"][:, :], QT)
            sc_st = contextlib.ExitStack()
            pL2 = [ps("a_pL", (128, 2, QT), sc_st) for _ in range(2)]
            pD = ps("a_pD", (128, QT), sc_st)
            pY = ps("a_pY", (128, 2, QT), sc_st)
            psums = ps("a_ps", (128, QT), sc_st)
            Gb = [sb("a_G", (128, QT), BF16, sc_st) for _ in range(2)]
            Eb = [sb("a_E", (128, 2, QT), BF16, sc_st) for _ in range(2)]
            Yraw = sb("a_Yr", (128, 2, QT), F32, sc_st)
            y0 = tmpf
            ysq = sb("a_ysq", (128, QT), BF16, sc_st)
            rbA = sb("a_rb", (128, 2, QT), BF16, sc_st)
            rstdb = sb("a_rstdb", (128, QT), BF16, sc_st)
            rrb = sb("a_rrb", (128, QT), BF16, sc_st)
            rrow2 = rrow
            a_sc = 4 * QT * hg

            gc0 = b.cnt["s_g"]
            lm0 = b.cnt["s_lm"]
            dm0 = b.cnt["s_dm"]
            pa0 = b.cnt["s_pa"]
            x0 = b.cnt["s_x"]
            pv, sq = [], []

            def at(lst, i):
                return lst[i] if 0 <= i < len(lst) else 0

            def aff(base, i):
                return base + i + 1 if i >= 0 else base

            prev = None
            b.wait("tensor", "s_ve")
            b.wait("tensor", "s_gd")
            insts = [(4 * hg + hl, t_) for hl in range(4) for t_ in range(NQT)]
            for xi, (h, t_) in enumerate(insts):
                hl = h - 4 * hg
                tsl = slice(QT * t_, QT * (t_ + 1))
                vs_off = 128 * hl
                for j in range(NKC):
                    jg = xi * NKC + j
                    sB = jg & 1
                    jsl = slice(128 * j, 128 * (j + 1))
                    b.wait("tensor", "s_x", aff(x0, jg - 2))
                    P.matmul(pL2[sB][:, 0, :], Kh[hl][0:64, jsl], Qh[h][0:64, tsl],
                             start=True, stop=True, tile_position=(0, 0))
                    P.matmul(pL2[sB][:, 1, :], Kh[hl][64:128, jsl],
                             Qh[h][64:128, tsl], start=True, stop=True,
                             tile_position=(64, 0), inc="s_lm")
                    b.wait("tensor", "s_g", aff(gc0, jg - 1))
                    if j == 7 and prev is not None:
                        b.wait("tensor", "s_sq", at(sq, xi - 1))
                    P.matmul(pD[:, :], Kn2[hl][:, jsl], Qh[h][:, tsl],
                             start=True, stop=True, tile_position=(0, 0),
                             inc="s_dm")
                    if j == 6 and prev is not None:
                        b.wait("tensor", "s_g", gc0 + jg + 1)
                        b.wait("tensor", "s_ve")
                        P.matmul(pD[0:1, :], ones_col[:, 0:1], ysq[:],
                                 start=True, stop=True, tile_position=(0, 0),
                                 inc="s_pv")
                    if j >= 1:
                        b.wait("tensor", "s_x", aff(x0, jg - 1))
                        if j == 1:
                            b.wait("tensor", "s_ev")
                        sBp = (jg - 1) & 1
                        jp = j - 1
                        P.matmul(pY[:, 0, :],
                                 _ap(Vtm[jp].ap(), vs_off, [[QT, 128], [1, 128]]),
                                 Eb[sBp][:, 0, :], start=(jp == 0),
                                 stop=(jp == NKC - 1))
                        P.matmul(pY[:, 1, :],
                                 _ap(Vtm[jp].ap(), vs_off, [[QT, 128], [1, 128]]),
                                 Eb[sBp][:, 1, :], start=(jp == 0),
                                 stop=(jp == NKC - 1))
                        P.matmul(psums[0:1, :], ones_col[:, 0:1], Eb[sBp][:, 0, :],
                                 start=(jp == 0), stop=(jp == NKC - 1),
                                 tile_position=(0, 0))
                        P.matmul(psums[32:33, :], ones_col[:, 0:1], Eb[sBp][:, 1, :],
                                 start=(jp == 0), stop=(jp == NKC - 1),
                                 tile_position=(0, 32), inc="s_pv")
                        pv.append(b.cnt["s_pv"])
                    b.wait("scalar", "s_dm", aff(dm0, jg))
                    b.wait("scalar", "s_pa", aff(pa0, jg - 2))
                    A.activation(Gb[sB][:], pD[:, :], AF.Tanh, scale=gvec[:, h:h + 1],
                                 inc="s_g")

                    if j == 6 and prev is not None:
                        b.wait("scalar", "s_pv")
                        A.activation(rrow2[0:1, 0:QT], pD[0:1, :], AF.Sqrt,
                                     scale=1.0 / 128.0, bias=ccol[0:1, 0:1],
                                     inc="s_sq")
                        while len(sq) < xi:
                            sq.append(b.cnt["s_sq"])
                    b.wait("scalar", "s_pa", aff(pa0, jg))
                    b.wait("scalar", "s_pv", at(pv, jg - 2))
                    A.activation(Eb[sB][:, :, :], pL2[sB][:, 0:2, :], AF.Exp,
                                 bias=ccol[0:1, 0:1] if False else ccol[:, 1:2],
                                 inc="s_x")

                    b.wait("vector", "s_g", aff(gc0, jg))
                    b.wait("vector", "s_lm", aff(lm0, jg))
                    V.tensor_add(pL2[sB][:, 0, :], pL2[sB][:, 0, :], Gb[sB][:])
                    V.tensor_sub(pL2[sB][:, 1, :], pL2[sB][:, 1, :], Gb[sB][:],
                                 inc="s_pa")
                    if prev is not None:
                        ph, pt = prev
                        ptsl = slice(QT * pt, QT * (pt + 1))
                        if j == 4:
                            b.wait("vector", "s_gd")
                            V.tensor_mul(y0[:], Yraw[:, 0, :], rbA[:, 0, :])
                            V.tensor_mul(tmpg[:, 0:QT], Yraw[:, 1, :], rbA[:, 1, :])
                            b.drain("vector")
                            V.tensor_sub(y0[:], y0[:], tmpg[:, 0:QT])
                            b.drain("vector")
                            V.tensor_mul(ysq[:], y0[:], y0[:])
                            b.drain("vector")
                            b.inc_last("vector", "s_ve")
                        if j == 8:
                            b.wait("vector", "s_sq", at(sq, xi - 1))
                            V.reciprocal(rrb[64:65, 0:QT], rrow2[0:1, 0:QT])
                            b.drain("vector")
                            b.inc_last("vector", "s_ve")
                            b.wait("gpsimd", "s_ve")
                            ld(G, d_scrb[a_sc + 2 * QT:a_sc + 3 * QT],
                               rrb[64:65, 0:QT], sem="s_gd")
                            b.wait("gpsimd", "s_gd")
                            ld(G, rstdb[:, :],
                               _ap(d_scrb, a_sc + 2 * QT, [[1, 1], [0, 128], [1, QT]]),
                               sem="s_gd")
                        if j == 11:
                            b.wait("vector", "s_gd")
                            V.tensor_mul(Qh[ph][:, ptsl], y0[:], rstdb[:, :])
                            b.drain("vector")
                            b.inc_last("vector", "s_ve")
                jg = xi * NKC + NKC - 1
                b.wait("tensor", "s_x", aff(x0, jg))
                sBp = jg & 1
                jp = NKC - 1
                P.matmul(pY[:, 0, :], _ap(Vtm[jp].ap(), vs_off, [[QT, 128], [1, 128]]),
                         Eb[sBp][:, 0, :], start=False, stop=True)
                P.matmul(pY[:, 1, :], _ap(Vtm[jp].ap(), vs_off, [[QT, 128], [1, 128]]),
                         Eb[sBp][:, 1, :], start=False, stop=True)
                P.matmul(psums[0:1, :], ones_col[:, 0:1], Eb[sBp][:, 0, :],
                         start=False, stop=True, tile_position=(0, 0))
                P.matmul(psums[32:33, :], ones_col[:, 0:1], Eb[sBp][:, 1, :],
                         start=False, stop=True, tile_position=(0, 32), inc="s_pv")
                pv.append(b.cnt["s_pv"])
                b.wait("vector", "s_pv", b.cnt["s_pv"])
                V.tensor_copy(Yraw[:, 0, :], pY[:, 0, :])
                V.tensor_copy(Yraw[:, 1, :], pY[:, 1, :])
                V.tensor_copy(rr[0:1, 0:QT], psums[0:1, :])
                V.tensor_copy(rr[32:33, 0:QT], psums[32:33, :])
                b.drain("vector")
                b.inc_last("vector", "s_ev")
                V.reciprocal(rrb[0:1, 0:QT], rr[0:1, 0:QT])
                V.reciprocal(rrb[32:33, 0:QT], rr[32:33, 0:QT])
                b.drain("vector")
                V.tensor_scalar_mul(rrb[32:33, 0:QT], rrb[32:33, 0:QT],
                                    lam_col[32:33, 0:1])
                b.drain("vector")
                b.inc_last("vector", "s_ve")
                b.wait("gpsimd", "s_ve")
                ld(G, d_scrb[a_sc:a_sc + QT], rrb[0:1, 0:QT], sem="s_gd")
                ld(G, d_scrb[a_sc + QT:a_sc + 2 * QT], rrb[32:33, 0:QT], sem="s_gd")
                b.wait("gpsimd", "s_gd")
                ld(G, rbA[:, :, :],
                   _ap(d_scrb, a_sc, [[1, 1], [0, 128], [1, 2 * QT]]), sem="s_gd")
                prev = (h, t_)
            ph, pt = prev
            ptsl = slice(QT * pt, QT * (pt + 1))
            b.wait("vector", "s_gd")
            V.tensor_mul(y0[:], Yraw[:, 0, :], rbA[:, 0, :])
            V.tensor_mul(tmpg[:, 0:QT], Yraw[:, 1, :], rbA[:, 1, :])
            b.drain("vector")
            V.tensor_sub(y0[:], y0[:], tmpg[:, 0:QT])
            b.drain("vector")
            V.tensor_mul(ysq[:], y0[:], y0[:])
            b.drain("vector")
            b.inc_last("vector", "s_ve")
            b.wait("tensor", "s_ve")
            P.matmul(pD[0:1, :], ones_col[:, 0:1], ysq[:], start=True, stop=True,
                     tile_position=(0, 0), inc="s_pv")
            b.wait("scalar", "s_pv")
            A.activation(rrow2[0:1, 0:QT], pD[0:1, :], AF.Sqrt, scale=1.0 / 128.0,
                         bias=ccol[0:1, 0:1], inc="s_sq")
            b.wait("vector", "s_sq")
            V.reciprocal(rrb[64:65, 0:QT], rrow2[0:1, 0:QT])
            b.drain("vector")
            b.inc_last("vector", "s_ve")
            b.wait("gpsimd", "s_ve")
            ld(G, d_scrb[a_sc + 2 * QT:a_sc + 3 * QT], rrb[64:65, 0:QT], sem="s_gd")
            b.wait("gpsimd", "s_gd")
            ld(G, rstdb[:, :], _ap(d_scrb, a_sc + 2 * QT, [[1, 1], [0, 128], [1, QT]]),
               sem="s_gd")
            b.wait("vector", "s_gd")
            V.tensor_mul(Qh[ph][:, ptsl], y0[:], rstdb[:, :])
            b.drain("vector")
            b.inc_last("vector", "s_ve")
            phase_barrier()
            sc_st.close()
            hst.close()
        xn_st.close()

        if dbg:
            for tt in range(NQT):
                tap("dbg_yn0", Qh[0][:, QT * tt:QT * (tt + 1)],
                    dbg_outs["dbg_yn0"][:, QT * tt:QT * (tt + 1)], QT)
        if n_phases >= 0.9:
            emit_proj_fm(di["WoT"], range(NCH), Qh, consumer_resid_acc(bo_col),
                         cast_scale=wsc_col[:, 0:1])
        phase_barrier()
        ast.close()

    def emit_ffn():
        fst = contextlib.ExitStack()
        xn = [sb("fxn", (128, T), BF16, fst) for c in range(NCH)]
        emit_ln(xn)
        fab = sb("fab", (128, NCH, 16), BF16, fst)
        fabf = sb("fabf", (128, NCH, 16), F32, fst)
        fbb = sb("fbb", (16, D), BF16, fst)
        fbbf = sb("fbbf", (16, D), F32, fst)
        lora1 = sb("lora1", (16, T), BF16, fst)
        pl = ps("f_pl", (128, QT), fst)
        b.wait("sync", "s_ve")
        for c in range(NCH):
            ld(S, fabf[:, c, :], di["fa"][c])
        ld(S, fbbf[:], di["fb"])
        b.wait("vector", "s_ld")
        V.tensor_copy(fab[:], fabf[:])
        V.tensor_copy(fbb[:], fbbf[:])
        b.drain("vector")
        b.inc_last("vector", "s_ve")
        b.wait("tensor", "s_ve")
        for t_ in range(NQT):
            b.wait("tensor", "s_ve")
            for c in range(NCH):
                P.matmul(pl[0:16, :], fab[:, c, :], xn[c][:, QT * t_:QT * (t_ + 1)],
                         start=(c == 0), stop=(c == NCH - 1))
            b.inc_last("tensor", "s_pe")
            b.wait("vector", "s_pe")
            V.tensor_copy(lora1[:, QT * t_:QT * (t_ + 1)], pl[0:16, :])
            b.drain("vector")
            b.inc_last("vector", "s_ve")
        wqb_s = sb("f_wqb", (128, D), BF16, fst)
        pfd = [ps("f_p", (128, QT), fst) for i in range(2)]
        b.wait("tensor", "s_gd", quant_gd)
        b.wait("tensor", "s_ve")
        b.wait("sync", "s_gd", quant_gd)
        for m in range(NCH):
            b.wait("sync", "s_pe")
            ld(S, wqb_s[:], _ap(d_wqb, m * D * 128,
                                [[128, 128], [128 * 128, NCH], [1, 128]]))
            b.wait("tensor", "s_ld")
            for t_ in range(NQT):
                bank = pfd[t_ % 2]
                tsl = slice(QT * t_, QT * (t_ + 1))
                b.wait("tensor", "s_ve")
                for c in range(NCH):
                    P.matmul(bank[:], wqb_s[:, 128 * c:128 * (c + 1)], xn[c][:, tsl],
                             start=(c == 0), stop=False)
                P.matmul(bank[:], fbb[0:16, 128 * m:128 * (m + 1)], lora1[0:16, tsl],
                         start=False, stop=True, inc="s_pe")
                b.wait("vector", "s_pe")
                V.scalar_tensor_tensor(resid[m][:, tsl], bank[:], fb_col[:, m:m + 1],
                                       resid[m][:, tsl], OP.add, OP.add)
                b.drain("vector")
                b.inc_last("vector", "s_ve")
            b.inc_last("tensor", "s_pe")
        phase_barrier()
        fst.close()

    if n_phases >= 2:
        emit_ffn()

    if n_phases >= 3:
        bst = contextlib.ExitStack()
        qB = [sb("bQ", (128, T), BF16, bst) for c in range(NCH)]
        cosb_b = sb("cosb_b", (128, T), BF16, bst)
        sinb_b = sb("sinb_b", (128, T), BF16, bst)
        xn_st = contextlib.ExitStack()
        xn = [sb("bxn", (128, T), BF16, xn_st) for c in range(NCH)]
        emit_ln(xn)

        rp_st = contextlib.ExitStack()
        cbf = sb("rp_cbf", (128, T), F32, rp_st)
        b.wait("sync", "s_ve")
        ld(S, cbf[:], di["cosb"])
        b.wait("vector", "s_ld")
        V.tensor_copy(cosb_b[:], cbf[:], inc="s_ve")
        b.wait("sync", "s_ve")
        ld(S, cbf[:], di["sinbs"])
        b.wait("vector", "s_ld")
        V.tensor_copy(sinb_b[:], cbf[:], inc="s_ve")
        b.wait("sync", "s_ve")
        rp_st.close()

        for hg in range(2):
            hst = contextlib.ExitStack()
            kB = [sb("bK", (128, T), BF16, hst) for i in range(4)]
            Vaug = [sb("bV", (128, 8, 65), BF16, hst) for j in range(NKC)]
            tb0 = [sb("b_tb0", (128, QT), BF16, hst) for _ in range(2)]
            pswap2 = [ps("b_pswap", (128, QT), hst) for _ in range(2)]

            rope_g = [0]

            def consumer_rope(dest_of):
                def consumer(m, t_, bank):
                    g2 = rope_g[0] % 2
                    rope_g[0] += 1
                    dest = dest_of(m)
                    tsl = slice(QT * t_, QT * (t_ + 1))
                    b.wait("vector", "s_pe")
                    V.tensor_copy(tb0[g2][:], bank[:])
                    b.drain("vector")
                    b.inc_last("vector", "s_ve")
                    b.wait("tensor", "s_ve")
                    P.matmul(pswap2[g2][:], perm_bf[:], tb0[g2][:], start=True,
                             stop=True, inc="s_pe")
                    b.wait("vector", "s_pe")
                    V.tensor_mul(tmpf[:, 0:QT], bank[:], cosb_b[:, tsl], inc="s_cons")
                    V.tensor_mul(tmpg[:, 0:QT], pswap2[g2][:], sinb_b[:, tsl])
                    b.drain("vector")
                    V.tensor_add(dest[:, tsl], tmpf[:, 0:QT], tmpg[:, 0:QT])
                    b.drain("vector")
                    b.inc_last("vector", "s_ve")
                return consumer

            emit_proj_fm(di["WqkT"], range(4 * hg, 4 * hg + 4), xn,
                         consumer_rope(lambda m: qB[m]))
            emit_proj_fm(di["WqkT"], range(8 + 4 * hg, 8 + 4 * hg + 4), xn,
                         consumer_rope(lambda m: kB[m - 8 - 4 * hg]))

            for j in range(NKC):
                V.memset(Vaug[j][:], 0.0)
            b.drain("vector")
            b.inc_last("vector", "s_ve")

            def evac_vb(j, half, pv):
                V.tensor_copy(_ap(Vaug[j].ap(), 0, [[520, 128], [65, 8], [1, 64]]),
                              _ap(pv.ap(), 0, [[QT, 128], [64, 8], [1, 64]]))

            emit_proj_tm(di["WvB"], xn, evac_vb, halves=[hg])
            for j in range(NKC):
                V.memset(_ap(Vaug[j].ap(), 64, [[520, 128], [65, 8], [1, 1]]), 1.0)
            b.drain("vector")
            b.inc_last("vector", "s_ve")

            sb_st = contextlib.ExitStack()
            pS2 = [ps("b_pS", (128, 2, QT), sb_st) for _ in range(2)]
            pYa = ps("b_pYa", (128, QT), sb_st)
            pYb = ps("b_pYb", (128, QT), sb_st)
            EbB = [sb("b_E", (128, 2, QT), BF16, sb_st) for _ in range(2)]
            YrawB = sb("b_Yr", (128, QT), F32, sb_st)
            rbB = sb("b_rb", (128, QT), BF16, sb_st)
            rrbB = sb("b_rrb", (128, QT), BF16, sb_st)
            a_sb = 2 * QT * (4 + hg)

            lm0 = b.cnt["s_lm"]
            x0 = b.cnt["s_x"]
            pv, msk = [], {}

            def at(lst, i):
                return lst[i] if 0 <= i < len(lst) else 0

            def aff(base, i):
                return base + i + 1 if i >= 0 else base

            b.wait("tensor", "s_ve")
            instsB = [(4 * hg + ci, t_) for ci in range(4) for t_ in range(NQT)]
            jgc = [0]
            prevB = None
            for xi, (c, t_) in enumerate(instsB):
                ci = c - 4 * hg
                tsl0 = QT * t_
                njc = 4 * t_ + 4
                jlist = list(range(njc))
                for j in jlist:
                    jg = jgc[0]
                    jgc[0] += 1
                    sB = jg & 1
                    q0 = max(0, 128 * j - tsl0)
                    jsl = slice(128 * j, 128 * (j + 1))
                    qsl = slice(tsl0 + q0, tsl0 + QT)
                    diag = (128 * j >= tsl0)
                    b.wait("tensor", "s_x", aff(x0, jg - 2))
                    P.matmul(pS2[sB][:, 0, q0:QT], kB[ci][0:64, jsl], qB[c][0:64, qsl],
                             start=True, stop=True, tile_position=(0, 0))
                    P.matmul(pS2[sB][:, 1, q0:QT], kB[ci][64:128, jsl],
                             qB[c][64:128, qsl], start=True, stop=True,
                             tile_position=(64, 0), inc="s_lm")
                    if j >= 1:
                        pj = j - 1
                        pq0 = max(0, 128 * pj - tsl0)
                        sBp = (jg - 1) & 1
                        wsem = "s_pa" if (jg - 1) in msk else "s_x"
                        wval = msk.get(jg - 1, aff(x0, jg - 1))
                        b.wait("tensor", wsem, wval)
                        if pj == 0:
                            b.wait("tensor", "s_ev")
                        P.matmul(pYa[0:65, pq0:QT], Vaug[pj][:, 2 * ci, :],
                                 EbB[sBp][:, 0, pq0:QT], start=(pj == 0),
                                 stop=(pj == njc - 1))
                        P.matmul(pYb[64:128, pq0:QT], Vaug[pj][:, 2 * ci + 1, 0:64],
                                 EbB[sBp][:, 1, pq0:QT], start=(pj == 0),
                                 stop=(pj == njc - 1), tile_position=(0, 64))
                        P.matmul(pYb[0:1, pq0:QT], ones_col[:, 0:1],
                                 EbB[sBp][:, 1, pq0:QT], start=(pj == 0),
                                 stop=(pj == njc - 1), tile_position=(0, 0),
                                 inc="s_pv")
                        pv.append(b.cnt["s_pv"])
                    b.wait("scalar", "s_lm", aff(lm0, jg))
                    b.wait("scalar", "s_pv", at(pv, jg - 2))
                    A.activation(EbB[sB][:, :, q0:QT], pS2[sB][:, :, q0:QT], AF.Exp,
                                 scale=0.125, bias=ccol[:, 2:3], inc="s_x")
                    if diag:
                        b.wait("vector", "s_x", aff(x0, jg))
                        V.tensor_mul(EbB[sB][:, 0, q0:q0 + 128],
                                     EbB[sB][:, 0, q0:q0 + 128], tri_bf[:, :])
                        V.tensor_mul(EbB[sB][:, 1, q0:q0 + 128],
                                     EbB[sB][:, 1, q0:q0 + 128], tri_bf[:, :],
                                     inc="s_pa")
                        msk[jg] = b.cnt["s_pa"]
                    if prevB is not None:
                        pc, pt = prevB
                        ptsl = slice(QT * pt, QT * (pt + 1))
                        if j == 3:
                            b.wait("vector", "s_gd")
                            V.tensor_mul(qB[pc][0:64, ptsl], YrawB[0:64, :],
                                         rbB[0:64, :])
                            V.tensor_mul(qB[pc][64:128, ptsl], YrawB[64:128, :],
                                         rbB[64:128, :])
                            b.drain("vector")
                            b.inc_last("vector", "s_ve")
                            prevB = None
                jg = jgc[0] - 1
                pj = njc - 1
                pq0 = max(0, 128 * pj - tsl0)
                sBp = jg & 1
                wsem = "s_pa" if jg in msk else "s_x"
                wval = msk.get(jg, aff(x0, jg))
                b.wait("tensor", wsem, wval)
                if pj == 0:
                    b.wait("tensor", "s_ev")
                P.matmul(pYa[0:65, pq0:QT], Vaug[pj][:, 2 * ci, :],
                         EbB[sBp][:, 0, pq0:QT], start=(pj == 0), stop=True)
                P.matmul(pYb[64:128, pq0:QT], Vaug[pj][:, 2 * ci + 1, 0:64],
                         EbB[sBp][:, 1, pq0:QT], start=(pj == 0), stop=True,
                         tile_position=(0, 64))
                P.matmul(pYb[0:1, pq0:QT], ones_col[:, 0:1], EbB[sBp][:, 1, pq0:QT],
                         start=(pj == 0), stop=True, tile_position=(0, 0),
                         inc="s_pv")
                pv.append(b.cnt["s_pv"])
                b.wait("vector", "s_pv", b.cnt["s_pv"])
                V.tensor_copy(YrawB[0:64, :], pYa[0:64, :])
                V.tensor_copy(YrawB[64:128, :], pYb[64:128, :])
                V.tensor_copy(rr[64:65, 0:QT], pYa[64:65, :])
                V.tensor_copy(rr[0:1, 0:QT], pYb[0:1, :])
                b.drain("vector")
                b.inc_last("vector", "s_ev")
                V.reciprocal(rrbB[64:65, 0:QT], rr[64:65, 0:QT])
                V.reciprocal(rrbB[0:1, 0:QT], rr[0:1, 0:QT])
                b.drain("vector")
                b.inc_last("vector", "s_ve")
                b.wait("gpsimd", "s_ve")
                ld(G, d_scrb[a_sb:a_sb + QT], rrbB[64:65, 0:QT], sem="s_gd")
                ld(G, d_scrb[a_sb + QT:a_sb + 2 * QT], rrbB[0:1, 0:QT], sem="s_gd")
                b.wait("gpsimd", "s_gd")
                ld(G, rbB[:, :], _ap(d_scrb, a_sb, [[QT, 2], [0, 64], [1, QT]]),
                   sem="s_gd")
                prevB = (c, t_)
            pc, pt = prevB
            ptsl = slice(QT * pt, QT * (pt + 1))
            b.wait("vector", "s_gd")
            V.tensor_mul(qB[pc][0:64, ptsl], YrawB[0:64, :], rbB[0:64, :])
            V.tensor_mul(qB[pc][64:128, ptsl], YrawB[64:128, :], rbB[64:128, :])
            b.drain("vector")
            b.inc_last("vector", "s_ve")
            phase_barrier()
            sb_st.close()
            hst.close()
        xn_st.close()

        emit_proj_fm(di["WoB"], range(NCH), qB, consumer_resid_acc(None))
        phase_barrier()
        bst.close()

    if n_phases >= 4:
        emit_ffn()

    b.wait("sync", "s_ve")
    for c in range(NCH):
        S.dma_start(out=d_out[c], in_=resid[c][:], inc="s_out", inc_k=16)
    b.wait("sync", "s_out")

    with nc.allow_low_precision(reason="bf16 softmax-scale broadcasts are within "
                                 "this problem's measured bf16-matmul tolerance"):
        with nc.Block() as block:
            b.emit(block)
    return nc


def _host_prep(inputs):
    f = np.float32

    def blocked_T(w):
        return np.ascontiguousarray(
            w.T.reshape(1024, w.shape[0] // 128, 128).transpose(1, 0, 2)).astype(f)

    def blocked(w):
        n_m = w.shape[1] // 128
        return np.ascontiguousarray(
            w.reshape(w.shape[0], n_m, 128).transpose(1, 0, 2)).astype(f)

    x = np.asarray(inputs["L0_attn_a_x"], f)
    com = {}
    com["WqT"] = blocked_T(np.asarray(inputs["L0_attn_a_Wq"], f))
    com["WkT"] = blocked_T(np.asarray(inputs["L0_attn_a_Wk"], f))
    com["WvT"] = np.ascontiguousarray(
        np.asarray(inputs["L0_attn_a_Wv"], f).T.reshape(8, 128, 1024))
    com["WoT"] = blocked_T(np.asarray(inputs["L0_attn_a_Wo"], f))
    com["bo"] = np.asarray(inputs["L0_attn_a_bo"], f)
    com["lamq1"] = np.asarray(inputs["L0_attn_a_lambda_q1"], f)
    com["lamk1"] = np.asarray(inputs["L0_attn_a_lambda_k1"], f)
    com["lamq2"] = np.asarray(inputs["L0_attn_a_lambda_q2"], f)
    com["lamk2"] = np.asarray(inputs["L0_attn_a_lambda_k2"], f)
    com["dgain"] = np.asarray(inputs["L0_attn_a_delta_gain"], f)
    com["chd"] = np.asarray(inputs["L0_attn_a_cos_head_delta"], f)
    com["clsr"] = np.asarray(inputs["L0_attn_a_cos_logit_scale_raw"], f).reshape(1)
    com["wsub"] = np.asarray(inputs["L0_attn_a_subln_weight"], f)
    wqkv = np.asarray(inputs["L0_attn_b_W_qkv"], f)
    com["WqkT"] = blocked(wqkv[:, 0:2048])
    com["WvB"] = np.ascontiguousarray(wqkv[:, 2048:3072].reshape(8, 128, 1024))
    com["WoB"] = blocked(np.asarray(inputs["L0_attn_b_W_o"], f))
    com["fker"] = np.ascontiguousarray(
        np.asarray(inputs["L0_ffn_kernel"], f).reshape(8, 128, 1024))
    com["fbias"] = np.asarray(inputs["L0_ffn_bias"], f)
    com["fa"] = np.ascontiguousarray(np.asarray(inputs["L0_ffn_a"], f).reshape(8, 128, 16))
    com["fb"] = np.asarray(inputs["L0_ffn_b"], f)

    half = 32
    theta = 1.0 / (10000.0 ** (np.arange(half, dtype=np.float64) / half))
    ang = np.arange(T, dtype=np.float64)[:, None] * theta[None, :]
    cos_ = np.cos(ang).T.astype(f)
    sin_ = np.sin(ang).T.astype(f)
    com["cosb"] = np.ascontiguousarray(np.tile(cos_, (4, 1)))
    sin_block = np.concatenate([-sin_, sin_], axis=0)
    com["sinbs"] = np.ascontiguousarray(np.tile(sin_block, (2, 1)))
    com["tri"] = np.ascontiguousarray(np.triu(np.ones((128, 128), f)))
    pm = np.zeros((128, 128), f)
    for i in range(128):
        pm[i ^ 32, i] = 1.0
    com["perm"] = pm

    in_maps = []
    for i in range(N_CORES):
        m = dict(com)
        m["xT"] = np.ascontiguousarray(x[i].T).reshape(8, 128, T)
        in_maps.append(m)
    return in_maps


def kernel(**inputs):
    in_maps = _host_prep(inputs)
    nc = build_program()
    res = run_bass_kernel_spmd(nc, in_maps, core_ids=list(range(N_CORES)))
    outs = []
    for i in range(N_CORES):
        o = np.asarray(res.results[i]["out"])
        outs.append(o.reshape(1024, T).T)
    return np.ascontiguousarray(np.stack(outs)).astype(np.float32)


# version 16 (final)
# speedup vs baseline: 1.0441x; 1.0006x over previous
import contextlib
import numpy as np
import ml_dtypes
import concourse.bass as bass
import concourse.mybir as mybir
from concourse.bass_utils import run_bass_kernel_spmd

F32 = mybir.dt.float32
BF16 = mybir.dt.bfloat16
AX = mybir.AxisListType.X
OP = mybir.AluOpType
AF = mybir.ActivationFunctionType

N_CORES = 8
T = 2048
D = 1024
NCH = 8
QT = 512
NQT = T // QT
NKC = T // 128
HA = 8
HB = 16
LAMBDA_INIT = 0.5
COS_MAX = 15.0
ALPHA = 0.5
RMS_EPS = 1e-5
LN_EPS = 1e-5
EXPA_SHIFT = -10.0
EXPB_SHIFT = -4.0

ENGINES = ("sync", "vector", "scalar", "tensor", "gpsimd")


class Rec:
    def __init__(self, b, eng):
        self._b = b
        self._eng = eng

    def __getattr__(self, name):
        b, eng = self._b, self._eng

        def call(*args, inc=None, inc_k=1, **kw):
            incs = []
            if inc is not None:
                b.cnt[inc] += inc_k
                incs.append((inc, inc_k))
            b.ops[eng].append((name, args, kw, incs))
        return call


class Builder:
    def __init__(self, nc, stack):
        self.nc = nc
        self.stack = stack
        self.ops = {e: [] for e in ENGINES}
        self.sem_handles = {}
        self.cnt = {}
        self.S = Rec(self, "sync")
        self.V = Rec(self, "vector")
        self.A = Rec(self, "scalar")
        self.P = Rec(self, "tensor")
        self.G = Rec(self, "gpsimd")

    def sem(self, name):
        self.sem_handles[name] = self.stack.enter_context(self.nc.semaphore(name))
        self.cnt[name] = 0

    def wait(self, eng, sem, val=None):
        v = self.cnt[sem] if val is None else val
        self.ops[eng].append(("wait_ge", (self.sem_handles[sem], v), {}, []))

    OWN_SEM = {"sync": "s_dn", "vector": "s_dv", "scalar": "s_ds",
               "tensor": "s_dt", "gpsimd": "s_dg"}

    def drain(self, eng):
        for entry in reversed(self.ops[eng]):
            if entry[0] == "wait_ge":
                return
            if entry[0] in ("drain", "nop"):
                continue
            if entry[3]:
                s0, _ = entry[3][0]
                self.wait(eng, s0)
                return
            own = self.OWN_SEM[eng]
            k = 16 if entry[0] == "dma_start" else 1
            self.cnt[own] += k
            entry[3].append((own, k))
            self.wait(eng, own)
            return

    def inc_last(self, eng, sem, k=1):
        for entry in reversed(self.ops[eng]):
            if entry[0] not in ("wait_ge", "drain", "nop"):
                if entry[3]:
                    s0, _ = entry[3][0]
                    self.wait(eng, s0)
                    self.cnt[sem] += k
                    self.ops[eng].append(("nop", (), {}, [(sem, k)]))
                    return
                k_eff = 16 if entry[0] == "dma_start" else k
                self.cnt[sem] += k_eff
                entry[3].append((sem, k_eff))
                return
        self.cnt[sem] += k
        self.ops[eng].append(("nop", (), {}, [(sem, k)]))

    def emit(self, block):
        sems = self.sem_handles

        def body_for(eng_name):
            oplist = self.ops[eng_name]

            def body(eng):
                for name, args, kw, incs in oplist:
                    if name == "wait_ge":
                        eng.wait_ge(*args)
                        continue
                    r = getattr(eng, name)(*args, **kw)
                    for s_, k_ in incs:
                        r.then_inc(sems[s_], k_)
            return body

        for name in ENGINES:
            getattr(block, name)(body_for(name))


def _rows(t, row0, rstep, nrows, col0, ncols):
    ap = t.ap() if hasattr(t, "ap") and not isinstance(t, bass.AP) else t
    pstride = ap.ap[0][0]
    return bass.AP(ap.tensor, row0 * pstride + col0,
                   [[rstep * pstride, nrows], [1, ncols]])


def _ap(t, offset, dims):
    return bass.AP(t.tensor if isinstance(t, bass.AP) else t, offset, dims)


_UID = [0]


def _u(name):
    _UID[0] += 1
    return f"{name}_{_UID[0]}"


def build_program(n_phases=99, dbg=False):
    _UID[0] = 0
    nc = bass.Bass("TRN2", target_bir_lowering=False, debug=False)
    st = contextlib.ExitStack()
    st.__enter__()
    b = Builder(nc, st)
    S, V, A, P, G = b.S, b.V, b.A, b.P, b.G

    di = {}

    def dram_in(name, shape, dtype=F32):
        di[name] = nc.dram_tensor(name, list(shape), dtype, kind="ExternalInput").ap()

    dram_in("xT", (NCH, 128, T))
    dram_in("WqT", (NCH, D, 128))
    dram_in("WkT", (NCH, D, 128))
    dram_in("WvT", (NCH, 128, D))
    dram_in("WoT", (NCH, D, 128))
    dram_in("bo", (D,))
    dram_in("lamq1", (64,))
    dram_in("lamk1", (64,))
    dram_in("lamq2", (64,))
    dram_in("lamk2", (64,))
    dram_in("dgain", (HA,))
    dram_in("chd", (HA,))
    dram_in("clsr", (1,))
    dram_in("wsub", (128,))
    dram_in("WqkT", (HB, D, 128))
    dram_in("WvB", (NCH, 128, D))
    dram_in("WoB", (NCH, D, 128))
    dram_in("fker", (NCH, 128, D))
    dram_in("fbias", (D,))
    dram_in("fa", (NCH, 128, 16))
    dram_in("fb", (16, D))
    dram_in("cosb", (128, T))
    dram_in("sinbs", (128, T))
    dram_in("tri", (128, 128))
    dram_in("perm", (128, 128))

    d_out = nc.dram_tensor("out", [NCH, 128, T], F32, kind="ExternalOutput").ap()
    dbg_outs = {}
    def mk_tap(name, shape):
        dbg_outs[name] = nc.dram_tensor(name, list(shape), F32,
                                        kind="ExternalOutput").ap()
    if dbg:
        for nm in ("dbg_xn", "dbg_q0", "dbg_k0", "dbg_yn0"):
            mk_tap(nm, (128, T))
        mk_tap("dbg_v0", (128, QT))
        mk_tap("dbg_e0", (128, 2, QT))
        mk_tap("dbg_sums", (128, QT))
    d_scr = nc.dram_tensor("scr", [1 << 17], F32, kind="Internal").ap()
    d_scrb = nc.dram_tensor("scrb", [1 << 15], BF16, kind="Internal").ap()
    d_wqb = nc.dram_tensor("wqb", [NCH, D, 128], BF16, kind="Internal").ap()

    for s in ("s_l0", "s_l1", "s_l2", "s_l3",
              "s_ld", "s_pe", "s_ve", "s_se", "s_gd", "s_out", "s_bar",
              "s_lm", "s_dm", "s_g", "s_pa", "s_x", "s_pv", "s_ev", "s_sq",
              "s_cons", "s_wc", "s_sr",
              "s_dn", "s_dv", "s_ds", "s_dt", "s_dg"):
        b.sem(s)

    def sb(name, shape, dtype, stack=None):
        return (stack or st).enter_context(nc.sbuf_tensor(_u(name), list(shape), dtype))

    def ps(name, shape, stack=None):
        return (stack or st).enter_context(nc.psum_tensor(_u(name), list(shape), F32))

    resid = [sb("resid", (128, T), F32) for c in range(NCH)]
    ones_col = sb("ones_col", (128, 1), BF16)
    ones_row = sb("ones_row", (1, 128), BF16)
    lam_row = sb("lam_row", (1, 2), F32)
    cos_row = sb("cos_row", (1, HA), F32)
    gvec = sb("gvec", (128, HA), F32)
    cvec8 = sb("cvec8", (128, HA), F32)
    bo_col = sb("bo_col", (128, NCH), F32)
    fb_col = sb("fb_col", (128, NCH), F32)
    wsc_col = sb("wsc_col", (128, 1), F32)
    lam_col = sb("lam_col", (128, 1), F32)
    tri_bf = sb("tri_bf", (128, 128), BF16)
    perm_bf = sb("perm_bf", (128, 128), BF16)
    rr = sb("rr", (128, 512), F32)
    rrow = sb("rrow", (1, 1024), F32)
    sgn_col = sb("sgn_col", (128, 1), F32)
    tmpf = sb("tmpf", (128, 512), F32)
    tmpg = sb("tmpg", (128, 512), F32)
    ccol = sb("ccol", (128, 4), F32)

    def ld(rec, dst, src, sem="s_ld", slow=False, fence=True):
        if fence and b.cnt[sem] > 0:
            b.wait(rec._eng, sem)
        if slow:
            rec.dma_start(out=dst, in_=src, inc=sem, inc_k=16,
                          allow_slow_non_contiguous=True)
        else:
            rec.dma_start(out=dst, in_=src, inc=sem, inc_k=16)

    scr_ptr = [0]

    def scr_alloc(n):
        a = scr_ptr[0]
        scr_ptr[0] += n
        assert scr_ptr[0] <= (1 << 16)
        return a

    dbg_stage = sb("dbg_stage", (128, 2 * QT), F32) if dbg else None

    def tap(name, src_ap, dst_ap, ncols, nrows=128):
        if not dbg:
            return
        b.wait("vector", "s_pe")
        b.wait("vector", "s_se")
        b.wait("vector", "s_gd")
        V.tensor_copy(dbg_stage[0:nrows, 0:ncols], src_ap)
        b.drain("vector")
        b.inc_last("vector", "s_ve")
        b.wait("gpsimd", "s_ve")
        ld(G, dst_ap, dbg_stage[0:nrows, 0:ncols], sem="s_gd")
        b.wait("gpsimd", "s_gd")

    def phase_barrier():
        for e in ENGINES:
            b.inc_last(e, "s_bar")
        for e in ENGINES:
            b.wait(e, "s_bar")

    c0_st = contextlib.ExitStack()
    srow = sb("srow", (1, 512), F32, c0_st)
    srow2 = sb("srow2", (1, 512), F32, c0_st)
    tri_f = sb("tri_f", (128, 128), F32, c0_st)
    perm_f = sb("perm_f", (128, 128), F32, c0_st)
    for c in range(NCH):
        ld(S if c % 2 == 0 else G, resid[c][:], di["xT"][c],
           sem=f"s_l{c % 4}")
    ld(S, tri_f[:], di["tri"])
    ld(S, perm_f[:], di["perm"])
    ld(S, bo_col[:, :], _ap(di["bo"], 0, [[1, 128], [128, NCH]]), slow=True)
    ld(S, fb_col[:, :], _ap(di["fbias"], 0, [[1, 128], [128, NCH]]), slow=True)
    ld(S, wsc_col[:, 0:1], di["wsub"])
    ld(S, srow[0:1, 0:64], di["lamq1"])
    ld(S, srow[0:1, 64:128], di["lamk1"])
    ld(S, srow[0:1, 128:192], di["lamq2"])
    ld(S, srow[0:1, 192:256], di["lamk2"])
    ld(S, srow[0:1, 256:264], di["chd"])
    ld(S, srow[0:1, 264:265], di["clsr"])
    ld(S, srow[0:1, 272:280], di["dgain"])

    b.wait("vector", "s_ld")
    V.tensor_mul(srow2[0:1, 0:64], srow[0:1, 0:64], srow[0:1, 64:128])
    V.tensor_mul(srow2[0:1, 64:128], srow[0:1, 128:192], srow[0:1, 192:256])
    b.drain("vector")
    V.reduce_sum(srow2[0:1, 128:129], srow2[0:1, 0:64], AX)
    V.reduce_sum(srow2[0:1, 129:130], srow2[0:1, 64:128], AX)
    b.drain("vector")
    b.inc_last("vector", "s_ve")
    b.wait("scalar", "s_ve")
    A.activation(srow2[0:1, 132:134], srow2[0:1, 128:130], AF.Exp, inc="s_se")
    b.wait("vector", "s_se")
    V.tensor_sub(lam_row[0:1, 0:1], srow2[0:1, 132:133], srow2[0:1, 133:134])
    b.drain("vector")
    V.tensor_scalar_add(lam_row[0:1, 0:1], lam_row[0:1, 0:1], LAMBDA_INIT)
    V.reduce_sum(srow2[0:1, 140:141], srow[0:1, 256:264], AX)
    b.drain("vector")
    V.tensor_scalar_mul(srow2[0:1, 140:141], srow2[0:1, 140:141], 1.0 / HA)
    b.drain("vector")
    V.tensor_scalar(srow2[0:1, 144:152], srow[0:1, 256:264], srow2[0:1, 140:141], None,
                    OP.subtract, inc="s_ve")
    b.wait("scalar", "s_ve")
    A.activation(srow2[0:1, 144:152], srow2[0:1, 144:152], AF.Tanh)
    b.drain("scalar")
    A.activation(srow2[0:1, 160:161], srow[0:1, 264:265], AF.Sigmoid, inc="s_se")
    b.wait("vector", "s_se")
    V.tensor_scalar(srow2[0:1, 144:152], srow2[0:1, 144:152], ALPHA, 1.0, OP.mult, OP.add)
    V.tensor_scalar_mul(srow2[0:1, 160:161], srow2[0:1, 160:161], COS_MAX)
    b.drain("vector")
    V.tensor_scalar(cos_row[0:1, :], srow2[0:1, 144:152], srow2[0:1, 160:161], None,
                    OP.mult)
    V.tensor_scalar_mul(wsc_col[:, 0:1], wsc_col[:, 0:1], 1.0 - LAMBDA_INIT)
    V.tensor_copy(tri_bf[:], tri_f[:])
    V.tensor_copy(perm_bf[:], perm_f[:])
    V.memset(ones_col[:], 1.0)
    V.memset(ones_row[:], 1.0)
    V.memset(sgn_col[0:64, 0:1], 1.0)
    V.memset(sgn_col[64:128, 0:1], -1.0)
    if dbg:
        V.memset(dbg_stage[:], 0.0)
    V.memset(ccol[:, 0:1], LN_EPS)
    V.memset(ccol[:, 1:2], EXPA_SHIFT)
    V.memset(ccol[:, 2:3], EXPB_SHIFT)
    b.drain("vector")
    b.inc_last("vector", "s_ve")

    a_lam = scr_alloc(32)
    b.wait("gpsimd", "s_ve")
    ld(G, d_scr[a_lam:a_lam + 1], lam_row[0:1, 0:1], sem="s_gd")
    ld(G, d_scr[a_lam + 2:a_lam + 10], srow[0:1, 272:280], sem="s_gd")
    ld(G, d_scr[a_lam + 12:a_lam + 20], cos_row[0:1, :], sem="s_gd")
    b.wait("gpsimd", "s_gd")
    ld(G, lam_col[:, 0:1], _ap(d_scr, a_lam, [[0, 128], [1, 1]]), sem="s_gd")
    ld(G, gvec[:, :], _ap(d_scr, a_lam + 2, [[0, 128], [1, HA]]), sem="s_gd")
    ld(G, cvec8[:, :], _ap(d_scr, a_lam + 12, [[0, 128], [1, HA]]), sem="s_gd")

    qst = contextlib.ExitStack()
    qf = [sb("qf", (128, D), F32, qst) for i in range(NCH)]
    qa = sb("qa", (128, D), F32, qst)
    qb_ = sb("qb", (128, D), F32, qst)
    qsg = sb("qsg", (128, D), F32, qst)
    qout = sb("qout", (128, D), BF16, qst)
    am_col = sb("am_col", (128, 4), F32, qst)
    tmpf2 = sb("qtmp2", (128, 512), F32, qst)

    for c in range(NCH):
        ld(S if c % 2 == 0 else G, qf[c][:], di["fker"][c], sem=f"s_l{c % 4}")
    for c in range(NCH):
        b.wait("vector", f"s_l{c % 4}")
        dst = am_col[:, 0:1] if c == 0 else am_col[:, 1:2]
        V.reduce_max(dst, qf[c][:], AX, apply_absolute_value=True)
        b.drain("vector")
        if c > 0:
            V.tensor_max(am_col[:, 0:1], am_col[:, 0:1], am_col[:, 1:2])
            b.drain("vector")
    b.inc_last("vector", "s_ve")
    a_am = scr_alloc(256)
    b.wait("gpsimd", "s_ve")
    ld(G, d_scr[a_am:a_am + 128], am_col[:, 0:1], sem="s_gd")
    b.wait("gpsimd", "s_gd")
    ld(G, srow2[0:1, 300:428], d_scr[a_am:a_am + 128], sem="s_gd")
    b.wait("vector", "s_gd")
    V.reduce_max(srow2[0:1, 430:431], srow2[0:1, 300:428], AX)
    b.drain("vector")
    V.tensor_scalar_mul(srow2[0:1, 431:432], srow2[0:1, 430:431], 1.0 / 7.0)
    b.drain("vector")
    V.reciprocal(srow2[0:1, 432:433], srow2[0:1, 431:432], inc="s_ve")
    b.wait("gpsimd", "s_ve")
    ld(G, d_scr[a_am + 130:a_am + 132], srow2[0:1, 431:433], sem="s_gd")
    b.wait("gpsimd", "s_gd")
    ld(G, am_col[:, 2:3], _ap(d_scr, a_am + 130, [[0, 128], [1, 1]]), sem="s_gd")
    ld(G, am_col[:, 3:4], _ap(d_scr, a_am + 131, [[0, 128], [1, 1]]), sem="s_gd")
    qcols_gd = b.cnt["s_gd"]
    for c in range(NCH):
        buf = qf[c]
        b.wait("vector", "s_gd")
        V.tensor_scalar_mul(qa[:], buf[:], am_col[:, 3:4], inc="s_ve")
        b.wait("scalar", "s_ve")
        A.sign(qsg[:], qa[:])
        b.drain("scalar")
        A.activation(qa[:], qa[:], AF.Abs, inc="s_se")
        b.wait("vector", "s_se")
        V.tensor_scalar(qb_[:], qa[:], 3.5, 4.0, OP.is_ge, OP.mult)
        b.drain("vector")
        V.tensor_sub(qa[:], qa[:], qb_[:])
        b.drain("vector")
        V.tensor_scalar(tmpf[:, 0:512], qa[:, 0:512], 1.5, 2.0, OP.is_ge, OP.mult)
        V.tensor_scalar(tmpf2[:, 0:512], qa[:, 512:1024], 1.5, 2.0, OP.is_ge, OP.mult)
        b.drain("vector")
        V.tensor_add(qb_[:, 0:512], qb_[:, 0:512], tmpf[:, 0:512])
        V.tensor_add(qb_[:, 512:1024], qb_[:, 512:1024], tmpf2[:, 0:512])
        V.tensor_sub(qa[:, 0:512], qa[:, 0:512], tmpf[:, 0:512])
        V.tensor_sub(qa[:, 512:1024], qa[:, 512:1024], tmpf2[:, 0:512])
        b.drain("vector")
        V.tensor_scalar(qa[:], qa[:], 0.5, None, OP.is_ge)
        b.drain("vector")
        V.tensor_add(qb_[:], qb_[:], qa[:])
        b.drain("vector")
        V.tensor_scalar(qb_[:], qb_[:], 7.0, -8.0, OP.min, OP.max)
        b.drain("vector")
        V.tensor_mul(qb_[:], qb_[:], qsg[:])
        b.drain("vector")
        V.tensor_scalar_mul(qout[:], qb_[:], am_col[:, 2:3], inc="s_ve")
        b.wait("gpsimd", "s_ve")
        for m in range(NCH):
            ld(G, d_wqb[m][128 * c:128 * (c + 1), :], qout[:, 128 * m:128 * (m + 1)],
               sem="s_gd")
    quant_gd = b.cnt["s_gd"]
    phase_barrier()
    qst.close()
    c0_st.close()

    def emit_ln(xn, cut=99):
        lst = contextlib.ExitStack()
        stat_st = contextlib.ExitStack()
        px = ps("ln_px", (128, NQT, QT), stat_st)
        pxq = ps("ln_pxq", (128, NQT, QT), stat_st)
        xb = [sb("ln_xb", (128, T), BF16, lst) for _ in range(2)]
        xq = [sb("ln_xq", (128, T), BF16, lst) for _ in range(2)]
        lnrows = sb("ln_rows", (1, 2 * T), BF16, lst)
        v0 = b.cnt["s_ve"]
        for c in range(NCH):
            if c >= 2:
                b.wait("vector", "s_pe")
            V.tensor_copy(xb[c % 2][:], resid[c][:])
            b.drain("vector")
            V.tensor_mul(xq[c % 2][:], xb[c % 2][:], xb[c % 2][:])
            b.drain("vector")
            b.inc_last("vector", "s_ve")
            b.wait("tensor", "s_ve", v0 + c + 1)
            last = (c == NCH - 1)
            for t_ in range(NQT):
                P.matmul(px[0:1, t_, :], ones_col[:, 0:1],
                         xb[c % 2][:, QT * t_:QT * (t_ + 1)], start=(c == 0),
                         stop=last, tile_position=(0, 0))
                P.matmul(pxq[0:1, t_, :], ones_col[:, 0:1],
                         xq[c % 2][:, QT * t_:QT * (t_ + 1)], start=(c == 0),
                         stop=last, tile_position=(0, 0))
            b.inc_last("tensor", "s_pe")
        b.wait("vector", "s_pe")
        if cut < 1:
            stat_st.close()
            lst.close()
            return
        for t_ in range(NQT):
            V.tensor_scalar_mul(rrow[0:1, 0:QT], px[0:1, t_, :], 1.0 / D)
            b.drain("vector")
            V.tensor_mul(rrow[0:1, QT:2 * QT], rrow[0:1, 0:QT], rrow[0:1, 0:QT])
            b.drain("vector")
            V.scalar_tensor_tensor(rrow[0:1, QT:2 * QT], pxq[0:1, t_, :], 1.0 / D,
                                   rrow[0:1, QT:2 * QT], OP.mult, OP.subtract)
            b.drain("vector")
            b.inc_last("vector", "s_ve")
            b.wait("scalar", "s_ve")
            A.activation(rrow[0:1, QT:2 * QT], rrow[0:1, QT:2 * QT], AF.Ln,
                         bias=ccol[0:1, 0:1])
            b.drain("scalar")
            A.activation(rrow[0:1, QT:2 * QT], rrow[0:1, QT:2 * QT], AF.Exp,
                         scale=-0.5, inc="s_se")
            b.wait("vector", "s_se")
            V.tensor_mul(rrow[0:1, 0:QT], rrow[0:1, 0:QT], rrow[0:1, QT:2 * QT])
            b.drain("vector")
            V.tensor_copy(lnrows[0:1, QT * t_:QT * (t_ + 1)], rrow[0:1, QT:2 * QT])
            V.tensor_copy(lnrows[0:1, T + QT * t_:T + QT * (t_ + 1)],
                          rrow[0:1, 0:QT])
            b.drain("vector")
            b.inc_last("vector", "s_ve")
        if cut < 2:
            stat_st.close()
            lst.close()
            return
        stat_st.close()
        pbc = ps("ln_bc", (128, 2, QT), lst)
        for t_ in range(NQT):
            b.wait("tensor", "s_ve")
            b.wait("tensor", "s_dv")
            P.matmul(pbc[:, 0, :], ones_row[0:1, :],
                     lnrows[0:1, QT * t_:QT * (t_ + 1)], start=True, stop=True)
            P.matmul(pbc[:, 1, :], ones_row[0:1, :],
                     lnrows[0:1, T + QT * t_:T + QT * (t_ + 1)], start=True,
                     stop=True, inc="s_pe")
            b.wait("vector", "s_pe")
            for c in range(NCH):
                sl = slice(QT * t_, QT * (t_ + 1))
                V.tensor_mul(tmpf[:, 0:QT], resid[c][:, sl], pbc[:, 0, :])
                b.drain("vector")
                V.tensor_sub(xn[c][:, sl], tmpf[:, 0:QT], pbc[:, 1, :])
                b.drain("vector")
            b.inc_last("vector", "s_dv")
        b.wait("vector", "s_dv")
        b.inc_last("vector", "s_ve")
        lst.close()

    def wdram_chunk(dram, m):
        return _ap(dram, m * D * 128, [[128, 128], [128 * 128, NCH], [1, 128]])

    def emit_proj_fm(wdram, m_list, xin, consumer, cast_scale=None):
        wst = contextlib.ExitStack()
        wf = [sb("pj_wf", (128, QT), F32, wst) for _ in range(2)]
        wb = [sb("pj_wb", (128, D), BF16, wst) for _ in range(2)]
        pbank = [ps("pj_p", (128, QT), wst) for _ in range(2)]
        c0 = b.cnt["s_cons"]
        wc0 = b.cnt["s_wc"]
        g = 0
        for mi, m in enumerate(m_list):
            for hh in range(2):
                hs = 2 * mi + hh
                if hs >= 2:
                    b.wait("sync", "s_wc", wc0 + hs - 1)
                ld(S, _ap(wf[hh].ap(), 0, [[QT, 128], [1, QT]]),
                   _ap(wdram, m * D * 128 + hh * QT * 128,
                       [[128, 128], [128 * 128, NCH // 2], [1, 128]]))
            b.wait("vector", "s_ld")
            b.wait("vector", "s_cons", c0 + 4 * (mi - 1) if mi >= 1 else c0)
            if cast_scale is None:
                V.tensor_copy(wb[mi % 2][:, 0:QT], wf[0][:], inc="s_wc")
                V.tensor_copy(wb[mi % 2][:, QT:D], wf[1][:], inc="s_wc")
            else:
                V.tensor_scalar_mul(wb[mi % 2][:, 0:QT], wf[0][:], cast_scale,
                                    inc="s_wc")
                V.tensor_scalar_mul(wb[mi % 2][:, QT:D], wf[1][:], cast_scale,
                                    inc="s_wc")
            for t_ in range(NQT):
                bank = pbank[g % 2]
                if t_ == 0:
                    b.wait("tensor", "s_wc", wc0 + 2 * (mi + 1))
                b.wait("tensor", "s_cons", c0 + g - 1 if g >= 1 else c0)
                for cc in range(NCH):
                    P.matmul(bank[:], wb[mi % 2][:, 128 * cc:128 * (cc + 1)],
                             xin[cc][:, QT * t_:QT * (t_ + 1)],
                             start=(cc == 0), stop=(cc == NCH - 1))
                b.inc_last("tensor", "s_pe")
                consumer(m, t_, bank)
                g += 1
        b.wait("sync", "s_cons")
        b.wait("tensor", "s_cons")
        b.wait("vector", "s_cons")
        wst.close()

    def consumer_resid_acc(bias_cols):
        def consumer(m, t_, bank):
            b.wait("vector", "s_pe")
            sl = slice(QT * t_, QT * (t_ + 1))
            if bias_cols is None:
                V.tensor_add(resid[m][:, sl], resid[m][:, sl], bank[:], inc="s_cons")
            else:
                V.scalar_tensor_tensor(resid[m][:, sl], bank[:], bias_cols[:, m:m + 1],
                                       resid[m][:, sl], OP.add, OP.add, inc="s_cons")
            b.drain("vector")
            b.inc_last("vector", "s_ve")
        return consumer

    def emit_proj_tm(wdram, xin, evac, halves):
        wst = contextlib.ExitStack()
        wvh = sb("tm_wvh", (128, NCH, QT), BF16, wst)
        wvhf = sb("tm_wvhf", (128, QT), F32, wst)
        pv2 = [ps("tm_pv", (128, QT), wst) for _ in range(2)]
        for half in halves:
            for cc in range(NCH):
                b.wait("sync", "s_wc")
                ld(S, wvhf[:], _ap(wdram, cc * 128 * D + half * QT, [[D, 128], [1, QT]]))
                b.wait("vector", "s_ld")
                V.tensor_copy(wvh[:, cc, :], wvhf[:], inc="s_wc")
            b.wait("tensor", "s_wc")
            c0 = b.cnt["s_cons"]
            for j in range(NKC):
                b.wait("tensor", "s_cons", c0 + j - 1 if j >= 1 else c0)
                for cc in range(NCH):
                    P.matmul(pv2[j % 2][:], xin[cc][:, 128 * j:128 * (j + 1)],
                             wvh[:, cc, :], start=(cc == 0), stop=(cc == NCH - 1))
                b.inc_last("tensor", "s_pe")
                b.wait("vector", "s_pe")
                evac(j, half, pv2[j % 2])
                b.drain("vector")
                b.inc_last("vector", "s_cons")
        b.wait("sync", "s_cons")
        b.wait("tensor", "s_cons")
        wst.close()

    if n_phases >= 0.2:
        ast = contextlib.ExitStack()
        Qh = [sb("aQ", (128, T), BF16, ast) for h in range(HA)]
        xn_st = contextlib.ExitStack()
        xn = [sb("xnA", (128, T), BF16, xn_st) for c in range(NCH)]
        emit_ln(xn, cut=int(round((n_phases - 0.2) * 100)) if n_phases < 0.3 else 99)

        if dbg:
            for tt in range(NQT):
                tap("dbg_xn", xn[0][:, QT * tt:QT * (tt + 1)],
                    dbg_outs["dbg_xn"][:, QT * tt:QT * (tt + 1)], QT)
        for hg in range(2 if n_phases >= 0.4 else 0):
            hst = contextlib.ExitStack()
            Kh = [sb("aK", (128, T), BF16, hst) for i in range(4)]
            Kn2 = [sb("aKn", (128, T), BF16, hst) for i in range(4)]
            Vtm = [sb("aV", (128, QT), BF16, hst) for j in range(NKC)]

            mrange = range(4 * hg, 4 * hg + 4)
            pj_st = contextlib.ExitStack()
            qsq = sb("qsq", (128, QT), BF16, pj_st)
            qrows = sb("qrows", (1, 4 * 2 * QT), BF16, pj_st)
            pssq2 = [ps("pssq", (128, 2, QT), pj_st) for _ in range(2)]
            pbcq = ps("pbcq", (128, QT), pj_st)
            b.wait("vector", "s_ld")

            def run_qk_proj(wdr, dest_of, with_c, neg_of):
                sr0 = b.cnt["s_sr"]
                gq = [0]

                def consumer(m, t_, bank):
                    g = gq[0]
                    sl = slice(QT * t_, QT * (t_ + 1))
                    dest = dest_of(m)
                    b.wait("vector", "s_pe")
                    V.tensor_copy(dest[:, sl], bank[:])
                    b.drain("vector")
                    V.tensor_mul(qsq[:], bank[:], dest[:, sl], inc="s_cons")
                    b.drain("vector")
                    b.inc_last("vector", "s_ve")
                    b.wait("tensor", "s_ve")
                    b.wait("tensor", "s_sr", sr0 + g - 1 if g >= 1 else sr0)
                    P.matmul(pssq2[g % 2][0:1, 0, :], ones_col[0:64, 0:1],
                             qsq[0:64, :], start=True, stop=True,
                             tile_position=(0, 0))
                    P.matmul(pssq2[g % 2][0:1, 1, :], ones_col[64:128, 0:1],
                             qsq[64:128, :], start=True, stop=True,
                             tile_position=(64, 0), inc="s_pe")
                    b.wait("vector", "s_pe")
                    V.tensor_copy(qrows[0:1, (g % 4) * 2 * QT:(g % 4 + 1) * 2 * QT],
                                  pssq2[g % 2][0:1, :, :])
                    b.drain("vector")
                    b.inc_last("vector", "s_sr")
                    gq[0] += 1
                    if t_ == NQT - 1:
                        b.inc_last("vector", "s_ve")
                        b.wait("scalar", "s_ve")
                        A.activation(qrows[0:1, :], qrows[0:1, :], AF.Ln)
                        b.drain("scalar")
                        A.activation(qrows[0:1, :], qrows[0:1, :], AF.Exp,
                                     scale=-0.5, inc="s_se")
                        b.wait("vector", "s_se")
                        V.tensor_scalar_min(qrows[0:1, :], qrows[0:1, :], 1e6)
                        b.drain("vector")
                        if with_c:
                            V.tensor_scalar_mul(qrows[0:1, :], qrows[0:1, :],
                                                cvec8[0:1, m:m + 1])
                            b.drain("vector")
                        b.inc_last("vector", "s_ve")
                        for ta in range(NQT):
                            ga = ta
                            asl = slice(QT * ta, QT * (ta + 1))
                            b.wait("tensor", "s_ve")
                            b.wait("tensor", "s_dv")
                            P.matmul(pbcq[0:64, :], ones_row[0:1, 0:64],
                                     qrows[0:1, ga * 2 * QT:ga * 2 * QT + QT],
                                     start=True, stop=True, tile_position=(0, 0))
                            P.matmul(pbcq[64:128, :], ones_row[0:1, 64:128],
                                     qrows[0:1, ga * 2 * QT + QT:(ga + 1) * 2 * QT],
                                     start=True, stop=True, tile_position=(0, 64),
                                     inc="s_pe")
                            b.wait("vector", "s_pe")
                            V.tensor_mul(dest[:, asl], dest[:, asl], pbcq[:, :])
                            b.drain("vector")
                            if neg_of is not None:
                                V.tensor_scalar_mul(neg_of(m)[:, asl], dest[:, asl],
                                                    sgn_col[:, 0:1])
                                b.drain("vector")
                            b.inc_last("vector", "s_dv")

                emit_proj_fm(wdr, mrange, xn, consumer)
                b.wait("vector", "s_dv")
                b.inc_last("vector", "s_ve")
                b.wait("tensor", "s_ve")

            run_qk_proj(di["WqT"], lambda m: Qh[m], True, None)
            run_qk_proj(di["WkT"], lambda m: Kh[m - 4 * hg], False,
                        lambda m: Kn2[m - 4 * hg])
            b.wait("sync", "s_ve")
            pj_st.close()
            if n_phases < 0.6:
                hst.close()
                continue

            def evac_va(j, half, pv):
                V.tensor_copy(Vtm[j][:, :], pv[:])

            emit_proj_tm(di["WvT"], xn, evac_va, halves=[hg])
            if n_phases < 0.8:
                hst.close()
                continue

            if dbg and hg == 0:
                for tt in range(NQT):
                    tap("dbg_q0", Qh[0][:, QT * tt:QT * (tt + 1)],
                        dbg_outs["dbg_q0"][:, QT * tt:QT * (tt + 1)], QT)
                    tap("dbg_k0", Kh[0][:, QT * tt:QT * (tt + 1)],
                        dbg_outs["dbg_k0"][:, QT * tt:QT * (tt + 1)], QT)
                tap("dbg_v0", Vtm[0][:, :], dbg_outs["dbg_v0"][:, :], QT)
            sc_st = contextlib.ExitStack()
            pL2 = [ps("a_pL", (128, 2, QT), sc_st) for _ in range(2)]
            pD = ps("a_pD", (128, QT), sc_st)
            pY = ps("a_pY", (128, 2, QT), sc_st)
            psums = ps("a_ps", (128, QT), sc_st)
            Gb = [sb("a_G", (128, QT), BF16, sc_st) for _ in range(2)]
            Eb = [sb("a_E", (128, 2, QT), BF16, sc_st) for _ in range(2)]
            Yraw = sb("a_Yr", (128, 2, QT), F32, sc_st)
            y0 = tmpf
            ysq = sb("a_ysq", (128, QT), BF16, sc_st)
            rbA = sb("a_rb", (128, 2, QT), BF16, sc_st)
            rstdb = sb("a_rstdb", (128, QT), BF16, sc_st)
            rrb = sb("a_rrb", (128, QT), BF16, sc_st)
            rrow2 = rrow
            a_sc = 4 * QT * hg

            gc0 = b.cnt["s_g"]
            lm0 = b.cnt["s_lm"]
            dm0 = b.cnt["s_dm"]
            pa0 = b.cnt["s_pa"]
            x0 = b.cnt["s_x"]
            pv, sq = [], []

            def at(lst, i):
                return lst[i] if 0 <= i < len(lst) else 0

            def aff(base, i):
                return base + i + 1 if i >= 0 else base

            prev = None
            b.wait("tensor", "s_ve")
            b.wait("tensor", "s_gd")
            insts = [(4 * hg + hl, t_) for hl in range(4) for t_ in range(NQT)]
            for xi, (h, t_) in enumerate(insts):
                hl = h - 4 * hg
                tsl = slice(QT * t_, QT * (t_ + 1))
                vs_off = 128 * hl
                for j in range(NKC):
                    jg = xi * NKC + j
                    sB = jg & 1
                    jsl = slice(128 * j, 128 * (j + 1))
                    b.wait("tensor", "s_x", aff(x0, jg - 2))
                    P.matmul(pL2[sB][:, 0, :], Kh[hl][0:64, jsl], Qh[h][0:64, tsl],
                             start=True, stop=True, tile_position=(0, 0))
                    P.matmul(pL2[sB][:, 1, :], Kh[hl][64:128, jsl],
                             Qh[h][64:128, tsl], start=True, stop=True,
                             tile_position=(64, 0), inc="s_lm")
                    b.wait("tensor", "s_g", aff(gc0, jg - 1))
                    if j == 7 and prev is not None:
                        b.wait("tensor", "s_sq", at(sq, xi - 1))
                    P.matmul(pD[:, :], Kn2[hl][:, jsl], Qh[h][:, tsl],
                             start=True, stop=True, tile_position=(0, 0),
                             inc="s_dm")
                    if j == 6 and prev is not None:
                        b.wait("tensor", "s_g", gc0 + jg + 1)
                        b.wait("tensor", "s_ve")
                        P.matmul(pD[0:1, :], ones_col[:, 0:1], ysq[:],
                                 start=True, stop=True, tile_position=(0, 0),
                                 inc="s_pv")
                    if j >= 1:
                        b.wait("tensor", "s_x", aff(x0, jg - 1))
                        if j == 1:
                            b.wait("tensor", "s_ev")
                        sBp = (jg - 1) & 1
                        jp = j - 1
                        P.matmul(pY[:, 0, :],
                                 _ap(Vtm[jp].ap(), vs_off, [[QT, 128], [1, 128]]),
                                 Eb[sBp][:, 0, :], start=(jp == 0),
                                 stop=(jp == NKC - 1))
                        P.matmul(pY[:, 1, :],
                                 _ap(Vtm[jp].ap(), vs_off, [[QT, 128], [1, 128]]),
                                 Eb[sBp][:, 1, :], start=(jp == 0),
                                 stop=(jp == NKC - 1))
                        P.matmul(psums[0:1, :], ones_col[:, 0:1], Eb[sBp][:, 0, :],
                                 start=(jp == 0), stop=(jp == NKC - 1),
                                 tile_position=(0, 0))
                        P.matmul(psums[32:33, :], ones_col[:, 0:1], Eb[sBp][:, 1, :],
                                 start=(jp == 0), stop=(jp == NKC - 1),
                                 tile_position=(0, 32), inc="s_pv")
                        pv.append(b.cnt["s_pv"])
                    b.wait("scalar", "s_dm", aff(dm0, jg))
                    b.wait("scalar", "s_pa", aff(pa0, jg - 2))
                    A.activation(Gb[sB][:], pD[:, :], AF.Tanh, scale=gvec[:, h:h + 1],
                                 inc="s_g")

                    if j == 6 and prev is not None:
                        b.wait("scalar", "s_pv")
                        A.activation(rrow2[0:1, 0:QT], pD[0:1, :], AF.Sqrt,
                                     scale=1.0 / 128.0, bias=ccol[0:1, 0:1],
                                     inc="s_sq")
                        while len(sq) < xi:
                            sq.append(b.cnt["s_sq"])
                    b.wait("scalar", "s_pa", aff(pa0, jg))
                    b.wait("scalar", "s_pv", at(pv, jg - 2))
                    A.activation(Eb[sB][:, :, :], pL2[sB][:, 0:2, :], AF.Exp,
                                 bias=ccol[0:1, 0:1] if False else ccol[:, 1:2],
                                 inc="s_x")

                    b.wait("vector", "s_g", aff(gc0, jg))
                    b.wait("vector", "s_lm", aff(lm0, jg))
                    V.tensor_add(pL2[sB][:, 0, :], pL2[sB][:, 0, :], Gb[sB][:])
                    V.tensor_sub(pL2[sB][:, 1, :], pL2[sB][:, 1, :], Gb[sB][:],
                                 inc="s_pa")
                    if prev is not None:
                        ph, pt = prev
                        ptsl = slice(QT * pt, QT * (pt + 1))
                        if j == 4:
                            b.wait("vector", "s_gd")
                            V.tensor_mul(y0[:], Yraw[:, 0, :], rbA[:, 0, :])
                            V.tensor_mul(tmpg[:, 0:QT], Yraw[:, 1, :], rbA[:, 1, :])
                            b.drain("vector")
                            V.tensor_sub(y0[:], y0[:], tmpg[:, 0:QT])
                            b.drain("vector")
                            V.tensor_mul(ysq[:], y0[:], y0[:])
                            b.drain("vector")
                            b.inc_last("vector", "s_ve")
                        if j == 8:
                            b.wait("vector", "s_sq", at(sq, xi - 1))
                            V.reciprocal(rrb[64:65, 0:QT], rrow2[0:1, 0:QT])
                            b.drain("vector")
                            b.inc_last("vector", "s_ve")
                            b.wait("gpsimd", "s_ve")
                            ld(G, d_scrb[a_sc + 2 * QT:a_sc + 3 * QT],
                               rrb[64:65, 0:QT], sem="s_gd")
                            b.wait("gpsimd", "s_gd")
                            ld(G, rstdb[:, :],
                               _ap(d_scrb, a_sc + 2 * QT, [[1, 1], [0, 128], [1, QT]]),
                               sem="s_gd")
                        if j == 11:
                            b.wait("vector", "s_gd")
                            V.tensor_mul(Qh[ph][:, ptsl], y0[:], rstdb[:, :])
                            b.drain("vector")
                            b.inc_last("vector", "s_ve")
                jg = xi * NKC + NKC - 1
                b.wait("tensor", "s_x", aff(x0, jg))
                sBp = jg & 1
                jp = NKC - 1
                P.matmul(pY[:, 0, :], _ap(Vtm[jp].ap(), vs_off, [[QT, 128], [1, 128]]),
                         Eb[sBp][:, 0, :], start=False, stop=True)
                P.matmul(pY[:, 1, :], _ap(Vtm[jp].ap(), vs_off, [[QT, 128], [1, 128]]),
                         Eb[sBp][:, 1, :], start=False, stop=True)
                P.matmul(psums[0:1, :], ones_col[:, 0:1], Eb[sBp][:, 0, :],
                         start=False, stop=True, tile_position=(0, 0))
                P.matmul(psums[32:33, :], ones_col[:, 0:1], Eb[sBp][:, 1, :],
                         start=False, stop=True, tile_position=(0, 32), inc="s_pv")
                pv.append(b.cnt["s_pv"])
                b.wait("vector", "s_pv", b.cnt["s_pv"])
                V.tensor_copy(Yraw[:, 0, :], pY[:, 0, :])
                V.tensor_copy(Yraw[:, 1, :], pY[:, 1, :])
                V.tensor_copy(rr[0:1, 0:QT], psums[0:1, :])
                V.tensor_copy(rr[32:33, 0:QT], psums[32:33, :])
                b.drain("vector")
                b.inc_last("vector", "s_ev")
                V.reciprocal(rrb[0:1, 0:QT], rr[0:1, 0:QT])
                V.reciprocal(rrb[32:33, 0:QT], rr[32:33, 0:QT])
                b.drain("vector")
                V.tensor_scalar_mul(rrb[32:33, 0:QT], rrb[32:33, 0:QT],
                                    lam_col[32:33, 0:1])
                b.drain("vector")
                b.inc_last("vector", "s_ve")
                b.wait("gpsimd", "s_ve")
                ld(G, d_scrb[a_sc:a_sc + QT], rrb[0:1, 0:QT], sem="s_gd")
                ld(G, d_scrb[a_sc + QT:a_sc + 2 * QT], rrb[32:33, 0:QT], sem="s_gd")
                b.wait("gpsimd", "s_gd")
                ld(G, rbA[:, :, :],
                   _ap(d_scrb, a_sc, [[1, 1], [0, 128], [1, 2 * QT]]), sem="s_gd")
                prev = (h, t_)
            ph, pt = prev
            ptsl = slice(QT * pt, QT * (pt + 1))
            b.wait("vector", "s_gd")
            V.tensor_mul(y0[:], Yraw[:, 0, :], rbA[:, 0, :])
            V.tensor_mul(tmpg[:, 0:QT], Yraw[:, 1, :], rbA[:, 1, :])
            b.drain("vector")
            V.tensor_sub(y0[:], y0[:], tmpg[:, 0:QT])
            b.drain("vector")
            V.tensor_mul(ysq[:], y0[:], y0[:])
            b.drain("vector")
            b.inc_last("vector", "s_ve")
            b.wait("tensor", "s_ve")
            P.matmul(pD[0:1, :], ones_col[:, 0:1], ysq[:], start=True, stop=True,
                     tile_position=(0, 0), inc="s_pv")
            b.wait("scalar", "s_pv")
            A.activation(rrow2[0:1, 0:QT], pD[0:1, :], AF.Sqrt, scale=1.0 / 128.0,
                         bias=ccol[0:1, 0:1], inc="s_sq")
            b.wait("vector", "s_sq")
            V.reciprocal(rrb[64:65, 0:QT], rrow2[0:1, 0:QT])
            b.drain("vector")
            b.inc_last("vector", "s_ve")
            b.wait("gpsimd", "s_ve")
            ld(G, d_scrb[a_sc + 2 * QT:a_sc + 3 * QT], rrb[64:65, 0:QT], sem="s_gd")
            b.wait("gpsimd", "s_gd")
            ld(G, rstdb[:, :], _ap(d_scrb, a_sc + 2 * QT, [[1, 1], [0, 128], [1, QT]]),
               sem="s_gd")
            b.wait("vector", "s_gd")
            V.tensor_mul(Qh[ph][:, ptsl], y0[:], rstdb[:, :])
            b.drain("vector")
            b.inc_last("vector", "s_ve")
            phase_barrier()
            sc_st.close()
            hst.close()
        xn_st.close()

        if dbg:
            for tt in range(NQT):
                tap("dbg_yn0", Qh[0][:, QT * tt:QT * (tt + 1)],
                    dbg_outs["dbg_yn0"][:, QT * tt:QT * (tt + 1)], QT)
        if n_phases >= 0.9:
            emit_proj_fm(di["WoT"], range(NCH), Qh, consumer_resid_acc(bo_col),
                         cast_scale=wsc_col[:, 0:1])
        phase_barrier()
        ast.close()

    def emit_ffn():
        fst = contextlib.ExitStack()
        xn = [sb("fxn", (128, T), BF16, fst) for c in range(NCH)]
        emit_ln(xn)
        fab = sb("fab", (128, NCH, 16), BF16, fst)
        fabf = sb("fabf", (128, NCH, 16), F32, fst)
        fbb = sb("fbb", (16, D), BF16, fst)
        fbbf = sb("fbbf", (16, D), F32, fst)
        lora1 = sb("lora1", (16, T), BF16, fst)
        pl = ps("f_pl", (128, QT), fst)
        b.wait("sync", "s_ve")
        for c in range(NCH):
            ld(S, fabf[:, c, :], di["fa"][c])
        ld(S, fbbf[:], di["fb"])
        b.wait("vector", "s_ld")
        V.tensor_copy(fab[:], fabf[:])
        V.tensor_copy(fbb[:], fbbf[:])
        b.drain("vector")
        b.inc_last("vector", "s_ve")
        b.wait("tensor", "s_ve")
        for t_ in range(NQT):
            b.wait("tensor", "s_ve")
            for c in range(NCH):
                P.matmul(pl[0:16, :], fab[:, c, :], xn[c][:, QT * t_:QT * (t_ + 1)],
                         start=(c == 0), stop=(c == NCH - 1))
            b.inc_last("tensor", "s_pe")
            b.wait("vector", "s_pe")
            V.tensor_copy(lora1[:, QT * t_:QT * (t_ + 1)], pl[0:16, :])
            b.drain("vector")
            b.inc_last("vector", "s_ve")
        wqb_s = sb("f_wqb", (128, D), BF16, fst)
        pfd = [ps("f_p", (128, QT), fst) for i in range(2)]
        b.wait("tensor", "s_gd", quant_gd)
        b.wait("tensor", "s_ve")
        b.wait("sync", "s_gd", quant_gd)
        for m in range(NCH):
            b.wait("sync", "s_pe")
            ld(S, wqb_s[:], _ap(d_wqb, m * D * 128,
                                [[128, 128], [128 * 128, NCH], [1, 128]]))
            b.wait("tensor", "s_ld")
            for t_ in range(NQT):
                bank = pfd[t_ % 2]
                tsl = slice(QT * t_, QT * (t_ + 1))
                b.wait("tensor", "s_ve")
                for c in range(NCH):
                    P.matmul(bank[:], wqb_s[:, 128 * c:128 * (c + 1)], xn[c][:, tsl],
                             start=(c == 0), stop=False)
                P.matmul(bank[:], fbb[0:16, 128 * m:128 * (m + 1)], lora1[0:16, tsl],
                         start=False, stop=True, inc="s_pe")
                b.wait("vector", "s_pe")
                V.scalar_tensor_tensor(resid[m][:, tsl], bank[:], fb_col[:, m:m + 1],
                                       resid[m][:, tsl], OP.add, OP.add)
                b.drain("vector")
                b.inc_last("vector", "s_ve")
            b.inc_last("tensor", "s_pe")
        phase_barrier()
        fst.close()

    if n_phases >= 2:
        emit_ffn()

    if n_phases >= 3:
        bst = contextlib.ExitStack()
        qB = [sb("bQ", (128, T), BF16, bst) for c in range(NCH)]
        cosb_b = sb("cosb_b", (128, T), BF16, bst)
        sinb_b = sb("sinb_b", (128, T), BF16, bst)
        xn_st = contextlib.ExitStack()
        xn = [sb("bxn", (128, T), BF16, xn_st) for c in range(NCH)]
        emit_ln(xn)

        rp_st = contextlib.ExitStack()
        cbf = sb("rp_cbf", (128, T), F32, rp_st)
        b.wait("sync", "s_ve")
        ld(S, cbf[:], di["cosb"])
        b.wait("vector", "s_ld")
        V.tensor_copy(cosb_b[:], cbf[:], inc="s_ve")
        b.wait("sync", "s_ve")
        ld(S, cbf[:], di["sinbs"])
        b.wait("vector", "s_ld")
        V.tensor_copy(sinb_b[:], cbf[:], inc="s_ve")
        b.wait("sync", "s_ve")
        rp_st.close()

        for hg in range(2):
            hst = contextlib.ExitStack()
            kB = [sb("bK", (128, T), BF16, hst) for i in range(4)]
            Vaug = [sb("bV", (128, 8, 65), BF16, hst) for j in range(NKC)]
            tb0 = [sb("b_tb0", (128, QT), BF16, hst) for _ in range(2)]
            pswap2 = [ps("b_pswap", (128, QT), hst) for _ in range(2)]

            rope_g = [0]

            def consumer_rope(dest_of):
                def consumer(m, t_, bank):
                    g2 = rope_g[0] % 2
                    rope_g[0] += 1
                    dest = dest_of(m)
                    tsl = slice(QT * t_, QT * (t_ + 1))
                    b.wait("vector", "s_pe")
                    V.tensor_copy(tb0[g2][:], bank[:])
                    b.drain("vector")
                    b.inc_last("vector", "s_ve")
                    b.wait("tensor", "s_ve")
                    P.matmul(pswap2[g2][:], perm_bf[:], tb0[g2][:], start=True,
                             stop=True, inc="s_pe")
                    b.wait("vector", "s_pe")
                    V.tensor_mul(tmpf[:, 0:QT], bank[:], cosb_b[:, tsl], inc="s_cons")
                    V.tensor_mul(tmpg[:, 0:QT], pswap2[g2][:], sinb_b[:, tsl])
                    b.drain("vector")
                    V.tensor_add(dest[:, tsl], tmpf[:, 0:QT], tmpg[:, 0:QT])
                    b.drain("vector")
                    b.inc_last("vector", "s_ve")
                return consumer

            emit_proj_fm(di["WqkT"], range(4 * hg, 4 * hg + 4), xn,
                         consumer_rope(lambda m: qB[m]))
            emit_proj_fm(di["WqkT"], range(8 + 4 * hg, 8 + 4 * hg + 4), xn,
                         consumer_rope(lambda m: kB[m - 8 - 4 * hg]))

            for j in range(NKC):
                V.memset(Vaug[j][:], 0.0)
            b.drain("vector")
            b.inc_last("vector", "s_ve")

            def evac_vb(j, half, pv):
                V.tensor_copy(_ap(Vaug[j].ap(), 0, [[520, 128], [65, 8], [1, 64]]),
                              _ap(pv.ap(), 0, [[QT, 128], [64, 8], [1, 64]]))

            emit_proj_tm(di["WvB"], xn, evac_vb, halves=[hg])
            for j in range(NKC):
                V.memset(_ap(Vaug[j].ap(), 64, [[520, 128], [65, 8], [1, 1]]), 1.0)
            b.drain("vector")
            b.inc_last("vector", "s_ve")

            sb_st = contextlib.ExitStack()
            pS2 = [ps("b_pS", (128, 2, QT), sb_st) for _ in range(2)]
            pYa = ps("b_pYa", (128, QT), sb_st)
            pYb = ps("b_pYb", (128, QT), sb_st)
            EbB = [sb("b_E", (128, 2, QT), BF16, sb_st) for _ in range(2)]
            YrawB = sb("b_Yr", (128, QT), F32, sb_st)
            rbB = sb("b_rb", (128, QT), BF16, sb_st)
            rrbB = sb("b_rrb", (128, QT), BF16, sb_st)
            a_sb = 2 * QT * (4 + hg)

            lm0 = b.cnt["s_lm"]
            x0 = b.cnt["s_x"]
            pv, msk = [], {}

            def at(lst, i):
                return lst[i] if 0 <= i < len(lst) else 0

            def aff(base, i):
                return base + i + 1 if i >= 0 else base

            b.wait("tensor", "s_ve")
            instsB = [(4 * hg + ci, t_) for ci in range(4) for t_ in range(NQT)]
            jgc = [0]
            prevB = None
            for xi, (c, t_) in enumerate(instsB):
                ci = c - 4 * hg
                tsl0 = QT * t_
                njc = 4 * t_ + 4
                jlist = list(range(njc))
                for j in jlist:
                    jg = jgc[0]
                    jgc[0] += 1
                    sB = jg & 1
                    q0 = max(0, 128 * j - tsl0)
                    jsl = slice(128 * j, 128 * (j + 1))
                    qsl = slice(tsl0 + q0, tsl0 + QT)
                    diag = (128 * j >= tsl0)
                    b.wait("tensor", "s_x", aff(x0, jg - 2))
                    P.matmul(pS2[sB][:, 0, q0:QT], kB[ci][0:64, jsl], qB[c][0:64, qsl],
                             start=True, stop=True, tile_position=(0, 0))
                    P.matmul(pS2[sB][:, 1, q0:QT], kB[ci][64:128, jsl],
                             qB[c][64:128, qsl], start=True, stop=True,
                             tile_position=(64, 0), inc="s_lm")
                    if j >= 1:
                        pj = j - 1
                        pq0 = max(0, 128 * pj - tsl0)
                        sBp = (jg - 1) & 1
                        wsem = "s_pa" if (jg - 1) in msk else "s_x"
                        wval = msk.get(jg - 1, aff(x0, jg - 1))
                        b.wait("tensor", wsem, wval)
                        if pj == 0:
                            b.wait("tensor", "s_ev")
                        P.matmul(pYa[0:65, pq0:QT], Vaug[pj][:, 2 * ci, :],
                                 EbB[sBp][:, 0, pq0:QT], start=(pj == 0),
                                 stop=(pj == njc - 1))
                        P.matmul(pYb[64:128, pq0:QT], Vaug[pj][:, 2 * ci + 1, 0:64],
                                 EbB[sBp][:, 1, pq0:QT], start=(pj == 0),
                                 stop=(pj == njc - 1), tile_position=(0, 64))
                        P.matmul(pYb[0:1, pq0:QT], ones_col[:, 0:1],
                                 EbB[sBp][:, 1, pq0:QT], start=(pj == 0),
                                 stop=(pj == njc - 1), tile_position=(0, 0),
                                 inc="s_pv")
                        pv.append(b.cnt["s_pv"])
                    b.wait("scalar", "s_lm", aff(lm0, jg))
                    b.wait("scalar", "s_pv", at(pv, jg - 2))
                    A.activation(EbB[sB][:, :, q0:QT], pS2[sB][:, :, q0:QT], AF.Exp,
                                 scale=0.125, bias=ccol[:, 2:3], inc="s_x")
                    if diag:
                        b.wait("vector", "s_x", aff(x0, jg))
                        V.tensor_mul(EbB[sB][:, 0, q0:q0 + 128],
                                     EbB[sB][:, 0, q0:q0 + 128], tri_bf[:, :])
                        V.tensor_mul(EbB[sB][:, 1, q0:q0 + 128],
                                     EbB[sB][:, 1, q0:q0 + 128], tri_bf[:, :],
                                     inc="s_pa")
                        msk[jg] = b.cnt["s_pa"]
                    if prevB is not None:
                        pc, pt = prevB
                        ptsl = slice(QT * pt, QT * (pt + 1))
                        if j == 3:
                            b.wait("vector", "s_gd")
                            V.tensor_mul(qB[pc][0:64, ptsl], YrawB[0:64, :],
                                         rbB[0:64, :])
                            V.tensor_mul(qB[pc][64:128, ptsl], YrawB[64:128, :],
                                         rbB[64:128, :])
                            b.drain("vector")
                            b.inc_last("vector", "s_ve")
                            prevB = None
                jg = jgc[0] - 1
                pj = njc - 1
                pq0 = max(0, 128 * pj - tsl0)
                sBp = jg & 1
                wsem = "s_pa" if jg in msk else "s_x"
                wval = msk.get(jg, aff(x0, jg))
                b.wait("tensor", wsem, wval)
                if pj == 0:
                    b.wait("tensor", "s_ev")
                P.matmul(pYa[0:65, pq0:QT], Vaug[pj][:, 2 * ci, :],
                         EbB[sBp][:, 0, pq0:QT], start=(pj == 0), stop=True)
                P.matmul(pYb[64:128, pq0:QT], Vaug[pj][:, 2 * ci + 1, 0:64],
                         EbB[sBp][:, 1, pq0:QT], start=(pj == 0), stop=True,
                         tile_position=(0, 64))
                P.matmul(pYb[0:1, pq0:QT], ones_col[:, 0:1], EbB[sBp][:, 1, pq0:QT],
                         start=(pj == 0), stop=True, tile_position=(0, 0),
                         inc="s_pv")
                pv.append(b.cnt["s_pv"])
                b.wait("vector", "s_pv", b.cnt["s_pv"])
                V.tensor_copy(YrawB[0:64, :], pYa[0:64, :])
                V.tensor_copy(YrawB[64:128, :], pYb[64:128, :])
                V.tensor_copy(rr[64:65, 0:QT], pYa[64:65, :])
                V.tensor_copy(rr[0:1, 0:QT], pYb[0:1, :])
                b.drain("vector")
                b.inc_last("vector", "s_ev")
                V.reciprocal(rrbB[64:65, 0:QT], rr[64:65, 0:QT])
                V.reciprocal(rrbB[0:1, 0:QT], rr[0:1, 0:QT])
                b.drain("vector")
                b.inc_last("vector", "s_ve")
                b.wait("gpsimd", "s_ve")
                ld(G, d_scrb[a_sb:a_sb + QT], rrbB[64:65, 0:QT], sem="s_gd")
                ld(G, d_scrb[a_sb + QT:a_sb + 2 * QT], rrbB[0:1, 0:QT], sem="s_gd")
                b.wait("gpsimd", "s_gd")
                ld(G, rbB[:, :], _ap(d_scrb, a_sb, [[QT, 2], [0, 64], [1, QT]]),
                   sem="s_gd")
                prevB = (c, t_)
            pc, pt = prevB
            ptsl = slice(QT * pt, QT * (pt + 1))
            b.wait("vector", "s_gd")
            V.tensor_mul(qB[pc][0:64, ptsl], YrawB[0:64, :], rbB[0:64, :])
            V.tensor_mul(qB[pc][64:128, ptsl], YrawB[64:128, :], rbB[64:128, :])
            b.drain("vector")
            b.inc_last("vector", "s_ve")
            phase_barrier()
            sb_st.close()
            hst.close()
        xn_st.close()

        emit_proj_fm(di["WoB"], range(NCH), qB, consumer_resid_acc(None))
        phase_barrier()
        bst.close()

    if n_phases >= 4:
        emit_ffn()

    b.wait("sync", "s_ve")
    for c in range(NCH):
        S.dma_start(out=d_out[c], in_=resid[c][:], inc="s_out", inc_k=16)
    b.wait("sync", "s_out")

    with nc.allow_low_precision(reason="bf16 softmax-scale broadcasts are within "
                                 "this problem's measured bf16-matmul tolerance"):
        with nc.Block() as block:
            b.emit(block)
    return nc


def _host_prep(inputs):
    f = np.float64 if np.asarray(inputs["L0_attn_a_x"]).dtype == np.float64 \
        else np.float32

    def blocked_T(w):
        return np.ascontiguousarray(
            w.T.reshape(1024, w.shape[0] // 128, 128).transpose(1, 0, 2)).astype(f)

    def blocked(w):
        n_m = w.shape[1] // 128
        return np.ascontiguousarray(
            w.reshape(w.shape[0], n_m, 128).transpose(1, 0, 2)).astype(f)

    x = np.asarray(inputs["L0_attn_a_x"], f)
    com = {}
    com["WqT"] = blocked_T(np.asarray(inputs["L0_attn_a_Wq"], f))
    com["WkT"] = blocked_T(np.asarray(inputs["L0_attn_a_Wk"], f))
    com["WvT"] = np.ascontiguousarray(
        np.asarray(inputs["L0_attn_a_Wv"], f).T.reshape(8, 128, 1024))
    com["WoT"] = blocked_T(np.asarray(inputs["L0_attn_a_Wo"], f))
    com["bo"] = np.asarray(inputs["L0_attn_a_bo"], f)
    com["lamq1"] = np.asarray(inputs["L0_attn_a_lambda_q1"], f)
    com["lamk1"] = np.asarray(inputs["L0_attn_a_lambda_k1"], f)
    com["lamq2"] = np.asarray(inputs["L0_attn_a_lambda_q2"], f)
    com["lamk2"] = np.asarray(inputs["L0_attn_a_lambda_k2"], f)
    com["dgain"] = np.asarray(inputs["L0_attn_a_delta_gain"], f)
    com["chd"] = np.asarray(inputs["L0_attn_a_cos_head_delta"], f)
    com["clsr"] = np.asarray(inputs["L0_attn_a_cos_logit_scale_raw"], f).reshape(1)
    com["wsub"] = np.asarray(inputs["L0_attn_a_subln_weight"], f)
    wqkv = np.asarray(inputs["L0_attn_b_W_qkv"], f)
    com["WqkT"] = blocked(wqkv[:, 0:2048])
    com["WvB"] = np.ascontiguousarray(wqkv[:, 2048:3072].reshape(8, 128, 1024))
    com["WoB"] = blocked(np.asarray(inputs["L0_attn_b_W_o"], f))
    com["fker"] = np.ascontiguousarray(
        np.asarray(inputs["L0_ffn_kernel"], f).reshape(8, 128, 1024))
    com["fbias"] = np.asarray(inputs["L0_ffn_bias"], f)
    com["fa"] = np.ascontiguousarray(np.asarray(inputs["L0_ffn_a"], f).reshape(8, 128, 16))
    com["fb"] = np.asarray(inputs["L0_ffn_b"], f)

    half = 32
    theta = 1.0 / (10000.0 ** (np.arange(half, dtype=np.float64) / half))
    ang = np.arange(T, dtype=np.float64)[:, None] * theta[None, :]
    cos_ = np.cos(ang).T.astype(f)
    sin_ = np.sin(ang).T.astype(f)
    com["cosb"] = np.ascontiguousarray(np.tile(cos_, (4, 1)))
    sin_block = np.concatenate([-sin_, sin_], axis=0)
    com["sinbs"] = np.ascontiguousarray(np.tile(sin_block, (2, 1)))
    com["tri"] = np.ascontiguousarray(np.triu(np.ones((128, 128), f)))
    pm = np.zeros((128, 128), f)
    for i in range(128):
        pm[i ^ 32, i] = 1.0
    com["perm"] = pm

    in_maps = []
    for i in range(N_CORES):
        m = dict(com)
        m["xT"] = np.ascontiguousarray(x[i].T).reshape(8, 128, T)
        in_maps.append(m)
    return in_maps


def kernel(**inputs):
    in_maps = _host_prep(inputs)
    nc = build_program()
    res = run_bass_kernel_spmd(nc, in_maps, core_ids=list(range(N_CORES)))
    outs = []
    for i in range(N_CORES):
        o = np.asarray(res.results[i]["out"])
        outs.append(o.reshape(1024, T).T)
    return np.ascontiguousarray(np.stack(outs)).astype(np.float32)
